# version 01
# baseline (speedup 1.0000x reference)
import os
import numpy as np
import concourse.bass as bass
import concourse.mybir as mybir
from concourse.bass_utils import run_bass_kernel_spmd

F32 = mybir.dt.float32
BF16 = mybir.dt.bfloat16
AF = mybir.ActivationFunctionType
ALU = mybir.AluOpType
AX = mybir.AxisListType

B = 8
NPOS = 2048
C = 1024
NC = 8
PP = NPOS // NC
T = B * PP
NCH = C // 128
NTT = T // 128
H = 16
DH = 64
E0 = 8
H0 = 4096
E1 = 8
FF1 = 4096
LN_EPS = 1e-5
GN_EPS = 0.00064

MODE = os.environ.get("MK_MODE", "unfused")


class _Op:
    __slots__ = ("eng", "fn", "dma", "idx", "dma_idx", "deps", "inc", "tag")

    def __init__(self, eng, fn, dma, tag):
        self.eng, self.fn, self.dma, self.tag = eng, fn, dma, tag
        self.deps = set()
        self.inc = False
        self.dma_idx = -1


class Sched:
    ENGINES = ("pe", "act", "dve", "pool", "sp")
    DMA_ENGINES = ("sp", "act", "pool")

    def __init__(self, nc):
        self.nc = nc
        self.eng_ops = {e: [] for e in self.ENGINES}
        self.last_w = {}
        self.readers = {}
        self.dma_count = {e: 0 for e in self.DMA_ENGINES}

    @staticmethod
    def _edge(dep, eng):
        if dep.dma:
            return True
        if dep.eng != eng:
            return True
        return eng != "pe"

    def op(self, eng, fn, reads=(), writes=(), dma=False, tag=""):
        o = _Op(eng, fn, dma, tag)
        if dma:
            assert eng in self.DMA_ENGINES
            o.dma_idx = self.dma_count[eng]
            self.dma_count[eng] += 1
        for r in reads:
            w = self.last_w.get(r)
            if w is not None and w is not o and self._edge(w, eng):
                o.deps.add(w)
            self.readers.setdefault(r, []).append(o)
        for wr in writes:
            for rd in self.readers.get(wr, ()):
                if rd is not o and self._edge(rd, eng):
                    o.deps.add(rd)
            w = self.last_w.get(wr)
            if w is not None and w is not o and self._edge(w, eng):
                o.deps.add(w)
            self.last_w[wr] = o
            self.readers[wr] = [o]
        self.eng_ops[eng].append(o)
        return o

    def emit(self):
        nc = self.nc
        for e in self.ENGINES:
            pass
        for ops in self.eng_ops.values():
            for o in ops:
                for d in o.deps:
                    if not d.dma:
                        d.inc = True
        inc_rank = {}
        for e, ops in self.eng_ops.items():
            k = 0
            for o in ops:
                if o.inc and not o.dma:
                    k += 1
                    inc_rank[id(o)] = k
        from contextlib import ExitStack
        with ExitStack() as st:
            esem = {e: st.enter_context(nc.semaphore(f"se_{e}"))
                    for e in self.ENGINES if any(not o.dma for o in self.eng_ops[e])}
            qsem = {e: st.enter_context(nc.semaphore(f"sq_{e}"))
                    for e in self.DMA_ENGINES if self.dma_count[e]}
            block = st.enter_context(nc.Block())
            engmap = {"pe": block.tensor, "act": block.scalar, "dve": block.vector,
                      "pool": block.gpsimd, "sp": block.sync}
            sched = self

            def mk_replay(ename):
                def replay(engobj):
                    watermark = {}
                    for o in sched.eng_ops[ename]:
                        waits = {}
                        for d in o.deps:
                            if d.dma:
                                sem, val = qsem[d.eng], 16 * (d.dma_idx + 1)
                            else:
                                sem, val = esem[d.eng], inc_rank[id(d)]
                            key = id(sem)
                            if val > waits.get(key, (None, 0))[1]:
                                waits[key] = (sem, val)
                        for sem, val in waits.values():
                            if val > watermark.get(id(sem), 0):
                                engobj.wait_ge(sem, val)
                                watermark[id(sem)] = val
                        inst = o.fn(engobj)
                        if o.dma:
                            assert inst is not None, f"dma op {o.tag} returned None"
                            inst.then_inc(qsem[o.eng], 16)
                        elif o.inc:
                            assert inst is not None, \
                                f"op {o.tag} needs inc but returned None"
                            inst.then_inc(esem[o.eng], 1)
                return replay

            replay_sp = mk_replay("sp")

            def sp_section(engobj):
                replay_sp(engobj)
                for q in self.DMA_ENGINES:
                    if self.dma_count[q]:
                        engobj.wait_ge(qsem[q], 16 * self.dma_count[q])

            for ename, dec in engmap.items():
                if ename == "sp":
                    dec(sp_section)
                elif self.eng_ops[ename]:
                    dec(mk_replay(ename))


SCRATCH = {
    "XFM":   ((NCH, 128, T), F32),
    "XNF":   ((NCH, 128, T), BF16),
    "ST0T":  ((2, 128, NTT), F32),
    "ST0R":  ((2, NTT, 128), F32),
    "QF":    ((NCH, 128, T), BF16),
    "KVZ":   ((B, H, DH, DH + 1), F32),
    "S0":    ((NCH, 128, T), BF16),
    "G0":    ((NCH, 128, T), BF16),
    "X1":    ((NCH, 128, T), F32),
    "XN1":   ((NCH, 128, T), BF16),
    "ST1T":  ((2, 128, NTT), F32),
    "ST1R":  ((2, NTT, 128), F32),
    "G1":    ((NCH, 128, T), BF16),
    "PRr":   ((NCH, 128, T), BF16),
    "PRw":   ((NCH, 128, T), BF16),
    "PRk":   ((NCH, 128, T), BF16),
    "PRv":   ((NCH, 128, T), BF16),
    "PRa":   ((NCH, 128, T), BF16),
    "PRg":   ((NCH, 128, T), BF16),
    "CMB0":  ((E0, T), F32),
    "CMB1":  ((E0, T), F32),
    "FFN0":  ((NCH, 128, T), F32),
    "FFN1":  ((NCH, 128, T), F32),
    "OUT":   ((B, PP, C), F32),
    "DBG_Y":   ((B, 128, T), F32),
    "DBG_YG":  ((B, 128, T), F32),
    "DBG_AT":  ((B, 128, T), F32),
    "DBG_ST":  ((B, 128, T), F32),
    "DBG_HT":  ((B, 128, T), F32),
    "DBG_SV":  ((B, 128, T), F32),
    "DBG_ROW": ((B, 1536), F32),
}
KVZALL = ("KVZALL", ((NC, B, H, DH, DH + 1), F32))


class Bufs:
    def __init__(self, nc, ext_in, ext_out):
        self.nc = nc
        self.ext_in = set(ext_in)
        self.ext_out = set(ext_out)
        self.cache = {}

    def get(self, name, spec=None):
        if name in self.cache:
            return self.cache[name]
        shape, dtype = spec if spec is not None else SCRATCH[name]
        if name in self.ext_in:
            kind = "ExternalInput"
        elif name in self.ext_out:
            kind = "ExternalOutput"
        else:
            kind = "Internal"
        ap = self.nc.dram_tensor(name, list(shape), dtype, kind=kind).ap()
        self.cache[name] = ap
        return ap


WEIGHT_SHAPES = {
    "L0_attn_W_qkv": (C, 3 * C), "L0_attn_W_out": (C, C), "L0_attn_b_out": (1, C),
    "L0_ffn_Wg": (C, E0), "L0_ffn_bg": (1, E0),
    "L0_ffn_W1": (E0, C, H0), "L0_ffn_b1": (E0, H0),
    "L0_ffn_W2": (E0, H0, C), "L0_ffn_b2": (E0, C),
    "L1_attn_Wr": (C, C), "L1_attn_Ww": (C, C), "L1_attn_Wk": (C, C),
    "L1_attn_Wv": (C, C), "L1_attn_Wa": (C, C), "L1_attn_Wg": (C, C),
    "L1_attn_Wo": (C, C), "L1_attn_Wf": (C, C),
    "L1_ffn_wrT": (C, E1),
    "L1_ffn_w1": (E1, C, 2 * FF1), "L1_ffn_b1": (E1, 2 * FF1),
    "L1_ffn_w2": (E1, FF1, C), "L1_ffn_b2": (E1, C),
    "VEC": (18, C),
    "IDENT": (128, 128),
}
VEC_ORDER = ["mu_r", "mu_w", "mu_k", "mu_v", "mu_a", "mu_g",
             "w0", "a0", "k_k", "k_a", "r_k",
             "lnx_w", "lnx_b", "bf", "lnf_w", "lnf_b", "norm_w", "norm_b"]
VI = {n: i for i, n in enumerate(VEC_ORDER)}


def host_prepare(inputs):
    vec = np.stack([np.asarray(inputs["L1_attn_" + n]) for n in VEC_ORDER]).astype(np.float32)
    common = {
        "L0_attn_W_qkv": inputs["L0_attn_W_qkv"],
        "L0_attn_W_out": inputs["L0_attn_W_out"],
        "L0_attn_b_out": np.asarray(inputs["L0_attn_b_out"]).reshape(1, C),
        "L0_ffn_Wg": inputs["L0_ffn_Wg"],
        "L0_ffn_bg": np.asarray(inputs["L0_ffn_bg"]).reshape(1, E0),
        "L0_ffn_W1": inputs["L0_ffn_W1"], "L0_ffn_b1": inputs["L0_ffn_b1"],
        "L0_ffn_W2": inputs["L0_ffn_W2"], "L0_ffn_b2": inputs["L0_ffn_b2"],
        "L1_attn_Wr": inputs["L1_attn_Wr"], "L1_attn_Ww": inputs["L1_attn_Ww"],
        "L1_attn_Wk": inputs["L1_attn_Wk"], "L1_attn_Wv": inputs["L1_attn_Wv"],
        "L1_attn_Wa": inputs["L1_attn_Wg"] if False else inputs["L1_attn_Wa"],
        "L1_attn_Wg": inputs["L1_attn_Wg"],
        "L1_attn_Wo": inputs["L1_attn_Wo"], "L1_attn_Wf": inputs["L1_attn_Wf"],
        "L1_ffn_wrT": np.ascontiguousarray(np.asarray(inputs["L1_ffn_wr"]).T),
        "L1_ffn_w1": inputs["L1_ffn_w1"], "L1_ffn_b1": inputs["L1_ffn_b1"],
        "L1_ffn_w2": inputs["L1_ffn_w2"], "L1_ffn_b2": inputs["L1_ffn_b2"],
        "VEC": vec,
        "IDENT": np.eye(128, dtype=np.float32),
    }
    common = {k: np.ascontiguousarray(np.asarray(v), dtype=None) for k, v in common.items()}
    x = np.asarray(inputs["L0_attn_x"])
    per_core = []
    for c in range(NC):
        m = dict(common)
        m["x_in"] = np.ascontiguousarray(x[:, c * PP:(c + 1) * PP, :])
        per_core.append(m)
    return per_core


class KB:

    def __init__(self, nc, bufs, want_inputs):
        self.nc = nc
        self.S = Sched(nc)
        self.bufs = bufs
        self.w = {}
        for name in want_inputs:
            shape = WEIGHT_SHAPES[name] if name in WEIGHT_SHAPES else None
            assert shape is not None, name
            self.w[name] = nc.dram_tensor(name, list(shape), F32,
                                          kind="ExternalInput").ap()
        self.ps = [nc.alloc_psum_tensor(f"psb{i}", [128, 512], F32) for i in range(8)]
        self._ps_rr = 0
        self._uid = 0

    def uid(self):
        self._uid += 1
        return self._uid

    def sb(self, name, shape, dtype):
        return self.nc.alloc_sbuf_tensor(name, list(shape), dtype)

    def ps_bank(self):
        i = self._ps_rr % 8
        self._ps_rr += 1
        return i

    def build_consts(self, need_ident=True, need_vec=False, need_b64=False,
                     ones_cols=512):
        S, nc = self.S, self.nc
        if need_ident:
            self.identF = self.sb("identF", [128, 128], F32)
            self.identB = self.sb("identB", [128, 128], BF16)
            idap = self.w["IDENT"]
            S.op("sp", lambda e: e.dma_start(out=self.identF[:], in_=idap),
                 writes=["identF"], dma=True, tag="ld_ident")
            S.op("dve", lambda e: e.tensor_copy(self.identB[:], self.identF[:]),
                 reads=["identF"], writes=["identB"], tag="cast_ident")
        self.onesF = self.sb("onesF", [128, ones_cols], F32)
        self.onesB = self.sb("onesB", [128, ones_cols], BF16)
        S.op("dve", lambda e: e.memset(self.onesF[:], 1.0), writes=["onesF"])
        S.op("dve", lambda e: e.memset(self.onesB[:], 1.0), writes=["onesB"])
        if need_b64:
            self.b64 = self.sb("b64", [128, 128], BF16)
            S.op("dve", lambda e: e.memset(self.b64[:], 0.0), writes=["b64"])
            S.op("dve", lambda e: e.memset(self.b64[0:64, 0:64], 1.0), writes=["b64"])
            S.op("dve", lambda e: e.memset(self.b64[64:128, 64:128], 1.0), writes=["b64"])
        if need_vec:
            self.vrow = self.sb("vrow", [18, C], F32)
            self.vcol = [self.sb(f"vcol{i}", [128, 32], F32) for i in range(NCH)]
            vap = self.w["VEC"]
            S.op("sp", lambda e: e.dma_start(out=self.vrow[:], in_=vap),
                 writes=["vrow"], dma=True, tag="ld_vec")
            for i in range(NCH):
                bk = self.ps_bank()
                S.op("pe", lambda e, i=i, bk=bk: e.transpose(
                    self.ps[bk][0:128, 0:18], self.vrow[:, i * 128:(i + 1) * 128],
                    self.identF[0:18, 0:18]),
                    reads=["vrow", "identF"], writes=[f"psb{bk}"], tag=f"tp_vec{i}")
                S.op("dve", lambda e, i=i, bk=bk: e.tensor_copy(
                    self.vcol[i][:, 0:18], self.ps[bk][0:128, 0:18]),
                    reads=[f"psb{bk}"], writes=[f"vcol{i}"], tag=f"ev_vec{i}")

    def vc(self, name, chunk):
        return self.vcol[chunk][:, VI[name]:VI[name] + 1]

    def kc(self, value):
        assert not getattr(self, "_emitting", False), "kc() called during emit"
        """Per-partition [128,1] constant AP (for activation bias/scale)."""
        if not hasattr(self, "_kc"):
            self._kc = self.sb("kconst", [128, 16], F32)
            self._kc_map = {}
        if value not in self._kc_map:
            j = len(self._kc_map)
            assert j < 16
            self._kc_map[value] = j
            self.S.op("dve", lambda e, j=j, value=value: e.memset(
                self._kc[:, j:j + 1], float(value)), writes=["kconst"],
                tag=f"kc_{value}")
        j = self._kc_map[value]
        return self._kc[:, j:j + 1]


def phaseA(kb):
    S, nc = kb.S, kb.nc
    x_in = nc.dram_tensor("x_in", [B, PP, C], F32, kind="ExternalInput").ap()
    XFM = kb.bufs.get("XFM")
    XNF = kb.bufs.get("XNF")
    ST0T = kb.bufs.get("ST0T")
    ST0R = kb.bufs.get("ST0R")

    eps_ap = kb.kc(LN_EPS)
    mhalf_ap = kb.kc(-0.5)
    TM = [kb.sb(f"A_tm{i}", [128, C], F32) for i in range(2)]
    XN_TM = [kb.sb(f"A_xntm{i}", [128, C], BF16) for i in range(2)]
    SQ = kb.sb("A_sq", [128, C], F32)
    STAT = kb.sb("A_stat", [128, 8], F32)
    STT = kb.sb("A_stt", [128, 2 * NTT], F32)
    XF_SB = [kb.sb(f"A_xf{i}", [128, C], F32) for i in range(2)]
    XN_SB = [kb.sb(f"A_xn{i}", [128, C], BF16) for i in range(2)]
    STR_SB = kb.sb("A_str", [128, 2 * 128], F32)

    for tt in range(NTT):
        b, half = tt // 2, tt % 2
        pb = tt % 2
        tm, xntm, xf, xn = TM[pb], XN_TM[pb], XF_SB[pb], XN_SB[pb]
        rtm, rxntm, rxf, rxn = f"A_tm{pb}", f"A_xntm{pb}", f"A_xf{pb}", f"A_xn{pb}"
        src = x_in[b, half * 128:half * 128 + 128, :]
        S.op("sp", lambda e, tm=tm, src=src: e.dma_start(out=tm[:], in_=src),
             writes=[rtm], dma=True, tag=f"A_ld{tt}")
        S.op("dve", lambda e, tm=tm: e.reduce_sum(STAT[:, 0:1], tm[:], axis=AX.X),
             reads=[rtm], writes=["A_stat"], tag=f"A_sum{tt}")
        S.op("act", lambda e, tm=tm: e.activation(SQ[:], tm[:], AF.Square),
             reads=[rtm], writes=["A_sq"], tag=f"A_sq{tt}")
        S.op("dve", lambda e: e.reduce_sum(STAT[:, 1:2], SQ[:], axis=AX.X),
             reads=["A_sq"], writes=["A_stat"], tag=f"A_ss{tt}")
        S.op("dve", lambda e: e.tensor_scalar(STAT[:, 2:3], STAT[:, 0:1],
                                              1.0 / C, None, ALU.mult),
             reads=["A_stat"], writes=["A_stat"], tag=f"A_mean{tt}")
        S.op("dve", lambda e: e.tensor_scalar(STAT[:, 3:4], STAT[:, 0:1],
                                              -1.0 / C, None, ALU.mult),
             reads=["A_stat"], writes=["A_stat"], tag=f"A_nm{tt}")
        S.op("act", lambda e: e.activation(STAT[:, 4:5], STAT[:, 2:3], AF.Square),
             reads=["A_stat"], writes=["A_stat"], tag=f"A_m2{tt}")
        S.op("dve", lambda e: e.scalar_tensor_tensor(
            STAT[:, 5:6], STAT[:, 1:2], 1.0 / C, STAT[:, 4:5],
            ALU.mult, ALU.subtract),
            reads=["A_stat"], writes=["A_stat"], tag=f"A_var{tt}")
        S.op("act", lambda e: e.activation(STAT[:, 6:7], STAT[:, 5:6], AF.Ln,
                                           bias=eps_ap),
             reads=["A_stat", "kconst"], writes=["A_stat"], tag=f"A_ln{tt}")
        S.op("act", lambda e: e.activation(STAT[:, 7:8], STAT[:, 6:7], AF.Exp,
                                           scale=mhalf_ap),
             reads=["A_stat", "kconst"], writes=["A_stat"], tag=f"A_rstd{tt}")
        S.op("dve", lambda e, tt=tt: e.tensor_copy(STT[:, tt:tt + 1], STAT[:, 3:4]),
             reads=["A_stat"], writes=["A_stt"], tag=f"A_stc{tt}")
        S.op("dve", lambda e, tt=tt: e.tensor_copy(STT[:, NTT + tt:NTT + tt + 1],
                                                   STAT[:, 7:8]),
             reads=["A_stat"], writes=["A_stt"], tag=f"A_stc2{tt}")
        S.op("dve", lambda e, tm=tm, xntm=xntm: e.tensor_scalar(
            xntm[:], tm[:], STAT[:, 3:4], STAT[:, 7:8], ALU.add, ALU.mult),
            reads=[rtm, "A_stat"], writes=[rxntm], tag=f"A_xn{tt}")
        for i in range(NCH):
            bk = kb.ps_bank()
            S.op("pe", lambda e, tm=tm, i=i, bk=bk: e.transpose(
                kb.ps[bk][:, 0:128], tm[:, i * 128:(i + 1) * 128], kb.identF[:]),
                reads=[rtm, "identF"], writes=[f"psb{bk}"], tag=f"A_tpx{tt}_{i}")
            S.op("dve", lambda e, xf=xf, i=i, bk=bk: e.tensor_copy(
                xf[:, i * 128:(i + 1) * 128], kb.ps[bk][:, 0:128]),
                reads=[f"psb{bk}"], writes=[rxf], tag=f"A_evx{tt}_{i}")
            bk2 = kb.ps_bank()
            psb = kb.ps[bk2][:, 0:64].bitcast(BF16)
            S.op("pe", lambda e, xntm=xntm, i=i, psb=psb: e.transpose(
                psb, xntm[:, i * 128:(i + 1) * 128], kb.identB[:]),
                reads=[rxntm, "identB"], writes=[f"psb{bk2}"], tag=f"A_tpn{tt}_{i}")
            S.op("act", lambda e, xn=xn, i=i, psb=psb: e.copy(
                xn[:, i * 128:(i + 1) * 128], psb),
                reads=[f"psb{bk2}"], writes=[rxn], tag=f"A_evn{tt}_{i}")
        dstx = XFM[:, :, tt * 128:(tt + 1) * 128].rearrange("i p n -> p i n")
        dstn = XNF[:, :, tt * 128:(tt + 1) * 128].rearrange("i p n -> p i n")
        S.op("sp", lambda e, xf=xf, dstx=dstx: e.dma_start(
            out=dstx, in_=xf[:].rearrange("p (i n) -> p i n", i=NCH)),
            reads=[rxf], writes=["XFM"], dma=True, tag=f"A_stx{tt}")
        S.op("act", lambda e, xn=xn, dstn=dstn: e.dma_start(
            out=dstn, in_=xn[:].rearrange("p (i n) -> p i n", i=NCH)),
            reads=[rxn], writes=["XNF"], dma=True, tag=f"A_stn{tt}")

    S.op("sp", lambda e: e.dma_start(
        out=ST0T.rearrange("s p n -> p s n"),
        in_=STT[:].rearrange("p (s n) -> p s n", s=2)),
        reads=["A_stt"], writes=["ST0T"], dma=True, tag="A_stt_st")
    for s in range(2):
        bk = kb.ps_bank()
        S.op("pe", lambda e, s=s, bk=bk: e.transpose(
            kb.ps[bk][0:NTT, 0:128], STT[:, s * NTT:(s + 1) * NTT], kb.identF[:]),
            reads=["A_stt", "identF"], writes=[f"psb{bk}"], tag=f"A_tps{s}")
        S.op("dve", lambda e, s=s, bk=bk: e.tensor_copy(
            STR_SB[0:NTT, s * 128:(s + 1) * 128], kb.ps[bk][0:NTT, 0:128]),
            reads=[f"psb{bk}"], writes=["A_str"], tag=f"A_evs{s}")
    S.op("sp", lambda e: e.dma_start(
        out=ST0R.rearrange("s t n -> t s n"),
        in_=STR_SB[0:NTT, :].rearrange("t (s n) -> t s n", s=2)),
        reads=["A_str"], writes=["ST0R"], dma=True, tag="A_str_st")


def phaseB(kb):
    S, nc = kb.S, kb.nc
    XNF = kb.bufs.get("XNF")
    QF = kb.bufs.get("QF")
    KVZ = kb.bufs.get("KVZ")
    Wq = kb.w["L0_attn_W_qkv"]

    XN = kb.sb("B_xn", [128, NCH * T], BF16)
    WQ = kb.sb("B_wq", [128, NCH * 3 * C], BF16)
    S.op("sp", lambda e: e.dma_start(
        out=XN[:].rearrange("p (i t) -> p i t", i=NCH),
        in_=XNF.rearrange("i p t -> p i t")),
        writes=["B_xn"], dma=True, tag="B_ldxn")
    S.op("pool", lambda e: e.dma_start(
        out=WQ[:].rearrange("p (i f) -> p i f", i=NCH),
        in_=Wq.rearrange("(i p) f -> p i f", i=NCH)),
        writes=["B_wq"], dma=True, tag="B_ldwq")

    TMPA = kb.sb("B_tmpa", [128, 512], F32)
    TMPB = kb.sb("B_tmpb", [128, 512], F32)
    QSB = [kb.sb(f"B_q{i}", [128, T], BF16) for i in range(2)]

    def elu1(prefix, psap, outap, rpsb, rout, n=512):
        S.op("dve", lambda e: e.tensor_scalar(TMPA[:, 0:n], psap, 0.0, None, ALU.min),
             reads=[rpsb], writes=["B_tmpa"], tag=prefix + "_min")
        S.op("act", lambda e: e.activation(TMPA[:, 0:n], TMPA[:, 0:n], AF.Exp),
             reads=["B_tmpa"], writes=["B_tmpa"], tag=prefix + "_exp")
        S.op("dve", lambda e: e.tensor_scalar(TMPB[:, 0:n], psap, 0.0, None, ALU.max),
             reads=[rpsb], writes=["B_tmpb"], tag=prefix + "_max")
        S.op("dve", lambda e: e.tensor_add(outap, TMPA[:, 0:n], TMPB[:, 0:n]),
             reads=["B_tmpa", "B_tmpb"], writes=[rout], tag=prefix + "_add")

    for f in range(NCH):
        qb = f % 2
        for t4 in range(4):
            bk = kb.ps_bank()
            ps = kb.ps[bk]
            for i in range(NCH):
                S.op("pe", lambda e, f=f, t4=t4, i=i, ps=ps: e.matmul(
                    ps[:, 0:512], WQ[:, i * 3 * C + f * 128: i * 3 * C + f * 128 + 128],
                    XN[:, i * T + t4 * 512: i * T + t4 * 512 + 512],
                    start=(i == 0), stop=(i == NCH - 1)),
                    reads=["B_wq", "B_xn"], writes=[f"psb{bk}"], tag=f"B_q{f}_{t4}_{i}")
            elu1(f"B_q{f}_{t4}", ps[:, 0:512], QSB[qb][:, t4 * 512:(t4 + 1) * 512],
                 f"psb{bk}", f"B_q{qb}")
        S.op("sp", lambda e, f=f, qb=qb: e.dma_start(
            out=QF[f], in_=QSB[qb][:]),
            reads=[f"B_q{qb}"], writes=["QF"], dma=True, tag=f"B_qst{f}")

    KT = kb.sb("B_kt", [128, 2 * C], BF16)
    VT = kb.sb("B_vt", [128, 2 * C], BF16)
    KVZ_SB = kb.sb("B_kvz", [128, 64 * 65], F32)
    for b in range(B):
        for tt in range(2):
            for j in range(4):
                bk = kb.ps_bank()
                ps = kb.ps[bk]
                for i in range(NCH):
                    S.op("pe", lambda e, b=b, tt=tt, j=j, i=i, ps=ps: e.matmul(
                        ps[:, 0:512],
                        XN[:, i * T + b * 256 + tt * 128: i * T + b * 256 + tt * 128 + 128],
                        WQ[:, i * 3 * C + C + j * 512: i * 3 * C + C + j * 512 + 512],
                        start=(i == 0), stop=(i == NCH - 1)),
                        reads=["B_xn", "B_wq"], writes=[f"psb{bk}"],
                        tag=f"B_kv{b}_{tt}_{j}_{i}")
                if j < 2:
                    elu1(f"B_k{b}_{tt}_{j}", ps[:, 0:512],
                         KT[:, tt * C + j * 512: tt * C + (j + 1) * 512],
                         f"psb{bk}", "B_kt")
                else:
                    S.op("act", lambda e, tt=tt, j=j, ps=ps: e.copy(
                        VT[:, tt * C + (j - 2) * 512: tt * C + (j - 1) * 512],
                        ps[:, 0:512]),
                        reads=[f"psb{bk}"], writes=["B_vt"], tag=f"B_v{b}_{tt}_{j}")
        for ph in range(8):
            bk = kb.ps_bank()
            ps = kb.ps[bk]
            first = True
            for tt in range(2):
                for hl in range(2):
                    h = 2 * ph + hl
                    S.op("pe", lambda e, h=h, hl=hl, tt=tt, ps=ps, first=first: e.matmul(
                        ps[hl * 64:hl * 64 + 64, 0:64],
                        KT[:, tt * C + h * 64: tt * C + h * 64 + 64],
                        VT[:, tt * C + h * 64: tt * C + h * 64 + 64],
                        start=first, stop=False),
                        reads=["B_kt", "B_vt"], writes=[f"psb{bk}"],
                        tag=f"B_kvm{b}_{ph}_{hl}_{tt}")
                    first = False
                    S.op("pe", lambda e, h=h, hl=hl, tt=tt, ps=ps, last=(tt == 1 and hl == 1): e.matmul(
                        ps[hl * 64:hl * 64 + 64, 64:65],
                        KT[:, tt * C + h * 64: tt * C + h * 64 + 64],
                        kb.onesB[:, 0:1],
                        start=False, stop=last),
                        reads=["B_kt", "onesB"], writes=[f"psb{bk}"],
                        tag=f"B_zm{b}_{ph}_{hl}_{tt}")
            S.op("dve", lambda e, b=b, ph=ph, ps=ps: e.tensor_copy(
                KVZ_SB[:, (b * 8 + ph) * 65:(b * 8 + ph + 1) * 65], ps[:, 0:65]),
                reads=[f"psb{bk}"], writes=["B_kvz"], tag=f"B_kvev{b}_{ph}")
    S.op("sp", lambda e: e.dma_start(
        out=KVZ.rearrange("b (ph hl) d e -> (hl d) b ph e", hl=2),
        in_=KVZ_SB[:].rearrange("p (b ph e) -> p b ph e", b=B, ph=8)),
        reads=["B_kvz"], writes=["KVZ"], dma=True, tag="B_kvzst")


def phaseC(kb, reduce_mode="stacked"):
    S, nc = kb.S, kb.nc
    QFb = kb.bufs.get("QF")
    S0 = kb.bufs.get("S0")
    WOap = kb.w["L0_attn_W_out"]
    BOap = kb.w["L0_attn_b_out"]

    KVA = kb.sb("C_kva", [128, 64 * 65], F32)
    KVL = kb.sb("C_kvl", [128, 64 * 65], F32)
    if reduce_mode == "stacked":
        KVZALL_ap = nc.dram_tensor("KVZALL", [NC, B, H, DH, DH + 1], F32,
                                   kind="ExternalInput").ap()
        for r in range(NC):
            src = KVZALL_ap[r].rearrange("b (ph hl) d e -> (hl d) b ph e", hl=2)
            if r == 0:
                S.op("sp", lambda e, src=src: e.dma_start(
                    out=KVA[:].rearrange("p (b ph e) -> p b ph e", b=B, ph=8),
                    in_=src), writes=["C_kva"], dma=True, tag="C_ld0")
            else:
                S.op("sp", lambda e, src=src: e.dma_start(
                    out=KVL[:].rearrange("p (b ph e) -> p b ph e", b=B, ph=8),
                    in_=src), writes=["C_kvl"], dma=True, tag=f"C_ld{r}")
                S.op("dve", lambda e: e.tensor_add(KVA[:], KVA[:], KVL[:]),
                     reads=["C_kva", "C_kvl"], writes=["C_kva"], tag=f"C_add{r}")
    else:
        raise NotImplementedError

    QS = kb.sb("C_q", [128, NCH * T], BF16)
    S.op("sp", lambda e: e.dma_start(
        out=QS[:].rearrange("p (i t) -> p i t", i=NCH),
        in_=QFb.rearrange("i p t -> p i t")),
        writes=["C_q"], dma=True, tag="C_ldq")
    WO = kb.sb("C_wo", [128, NCH * C], BF16)
    S.op("pool", lambda e: e.dma_start(
        out=WO[:].rearrange("p (i f) -> p i f", i=NCH),
        in_=WOap.rearrange("(i p) f -> p i f", i=NCH)),
        writes=["C_wo"], dma=True, tag="C_ldwo")
    BO = kb.sb("C_bo", [1, C], F32)
    BOB = kb.sb("C_bob", [1, C], BF16)
    S.op("sp", lambda e: e.dma_start(out=BO[:], in_=BOap),
         writes=["C_bo"], dma=True, tag="C_ldbo")
    S.op("dve", lambda e: e.tensor_copy(BOB[:], BO[:]),
         reads=["C_bo"], writes=["C_bob"], tag="C_cbo")

    BD = [kb.sb(f"C_bd{i}", [128, 128], BF16) for i in range(2)]
    Z2 = [kb.sb(f"C_z2{i}", [128, 2], BF16) for i in range(2)]
    DEN = kb.sb("C_den", [1, 1024], F32)
    RB = kb.sb("C_rb", [128, 256], F32)
    Y = kb.sb("C_y", [128, NCH * T], BF16)

    for j in range(NCH):
        for b in range(B):
            pp = (j * B + b) % 2
            col = (b * 8 + j) * 65
            S.op("dve", lambda e, pp=pp: e.memset(BD[pp][:], 0.0),
                 writes=[f"C_bd{pp}"], tag=f"C_bdz{j}_{b}")
            S.op("dve", lambda e, pp=pp, col=col: e.tensor_copy(
                BD[pp][0:64, 0:64], KVA[0:64, col:col + 64]),
                reads=["C_kva"], writes=[f"C_bd{pp}"], tag=f"C_bd0{j}_{b}")
            S.op("dve", lambda e, pp=pp, col=col: e.tensor_copy(
                BD[pp][64:128, 64:128], KVA[64:128, col:col + 64]),
                reads=["C_kva"], writes=[f"C_bd{pp}"], tag=f"C_bd1{j}_{b}")
            S.op("dve", lambda e, pp=pp: e.memset(Z2[pp][:], 0.0),
                 writes=[f"C_z2{pp}"], tag=f"C_z2z{j}_{b}")
            S.op("dve", lambda e, pp=pp, col=col: e.tensor_copy(
                Z2[pp][0:64, 0:1], KVA[0:64, col + 64:col + 65]),
                reads=["C_kva"], writes=[f"C_z2{pp}"], tag=f"C_z20{j}_{b}")
            S.op("dve", lambda e, pp=pp, col=col: e.tensor_copy(
                Z2[pp][64:128, 1:2], KVA[64:128, col + 64:col + 65]),
                reads=["C_kva"], writes=[f"C_z2{pp}"], tag=f"C_z21{j}_{b}")
            qsl = QS[:, j * T + b * 256: j * T + (b + 1) * 256]
            bkN = kb.ps_bank()
            bkD = kb.ps_bank()
            bkB = kb.ps_bank()
            S.op("pe", lambda e, pp=pp, qsl=qsl, bkN=bkN: e.matmul(
                kb.ps[bkN][:, 0:256], BD[pp][:], qsl, start=True, stop=True),
                reads=[f"C_bd{pp}", "C_q"], writes=[f"psb{bkN}"], tag=f"C_num{j}_{b}")
            for hl in range(2):
                S.op("pe", lambda e, pp=pp, qsl=qsl, bkD=bkD, hl=hl: e.matmul(
                    kb.ps[bkD][0:1, hl * 256:hl * 256 + 256],
                    Z2[pp][:, hl:hl + 1], qsl,
                    start=True, stop=True),
                    reads=[f"C_z2{pp}", "C_q"], writes=[f"psb{bkD}"],
                    tag=f"C_den{j}_{b}_{hl}")
            S.op("dve", lambda e, bkD=bkD: e.tensor_scalar(
                DEN[0:1, 0:512], kb.ps[bkD][0:1, 0:512], 1e-6, None, ALU.add),
                reads=[f"psb{bkD}"], writes=["C_den"], tag=f"C_de{j}_{b}")
            S.op("dve", lambda e: e.reciprocal(DEN[0:1, 512:1024], DEN[0:1, 0:512]),
                 reads=["C_den"], writes=["C_den"], tag=f"C_rec{j}_{b}")
            for hl in range(2):
                S.op("pe", lambda e, hl=hl, bkB=bkB: e.matmul(
                    kb.ps[bkB][hl * 64:hl * 64 + 64, 0:256],
                    kb.onesF[0:1, 0:64],
                    DEN[0:1, 512 + hl * 256: 512 + hl * 256 + 256],
                    start=True, stop=True),
                    reads=["C_den", "onesF"], writes=[f"psb{bkB}"],
                    tag=f"C_bc{j}_{b}_{hl}")
            S.op("act", lambda e, bkB=bkB: e.copy(RB[:], kb.ps[bkB][:, 0:256]),
                 reads=[f"psb{bkB}"], writes=["C_rb"], tag=f"C_rbe{j}_{b}")
            S.op("dve", lambda e, j=j, b=b, bkN=bkN: e.tensor_mul(
                Y[:, j * T + b * 256: j * T + (b + 1) * 256],
                kb.ps[bkN][:, 0:256], RB[:]),
                reads=[f"psb{bkN}", "C_rb"], writes=["C_y"], tag=f"C_y{j}_{b}")

    SSB = [kb.sb(f"C_s{i}", [128, T], BF16) for i in range(2)]
    for f in range(NCH):
        sb_ = f % 2
        for t4 in range(4):
            bk = kb.ps_bank()
            ps = kb.ps[bk]
            S.op("pe", lambda e, f=f, t4=t4, ps=ps: e.matmul(
                ps[:, 0:512], BOB[0:1, f * 128:(f + 1) * 128],
                kb.onesB[0:1, 0:512], start=True, stop=False),
                reads=["C_bob", "onesB"], writes=[f"psb{bk}"], tag=f"C_sb{f}_{t4}")
            for i in range(NCH):
                S.op("pe", lambda e, f=f, t4=t4, i=i, ps=ps: e.matmul(
                    ps[:, 0:512], WO[:, i * C + f * 128: i * C + f * 128 + 128],
                    Y[:, i * T + t4 * 512: i * T + t4 * 512 + 512],
                    start=False, stop=(i == NCH - 1)),
                    reads=["C_wo", "C_y"], writes=[f"psb{bk}"], tag=f"C_s{f}_{t4}_{i}")
            S.op("act", lambda e, sb_=sb_, t4=t4, ps=ps: e.copy(
                SSB[sb_][:, t4 * 512:(t4 + 1) * 512], ps[:, 0:512]),
                reads=[f"psb{bk}"], writes=[f"C_s{sb_}"], tag=f"C_sev{f}_{t4}")
        S.op("sp", lambda e, f=f, sb_=sb_: e.dma_start(
            out=S0[f], in_=SSB[sb_][:]),
            reads=[f"C_s{sb_}"], writes=["S0"], dma=True, tag=f"C_sst{f}")


def emit_routing(kb, pref, XSRC, STT_src, STR_src, WGap, BGap, cmb_name):
    S = kb.S
    STT = kb.sb(pref + "_stt", [128, 32], F32)
    STR = kb.sb(pref + "_str", [1, T], F32)
    WG = kb.sb(pref + "_wg", [128, NCH * E0], F32)
    CS = kb.sb(pref + "_cs", [1, E0], F32)
    COMBF = kb.sb(pref + "_combf", [E0, T], F32)
    rSTT, rSTR, rWG, rCS = pref + "_stt", pref + "_str", pref + "_wg", pref + "_cs"
    rCF = pref + "_combf"

    S.op("sp", lambda e: e.dma_start(
        out=STT[:].rearrange("p (s n) -> p s n", s=2),
        in_=STT_src.rearrange("s p n -> p s n")),
        writes=[rSTT], dma=True, tag=pref + "_ldstt")
    S.op("sp", lambda e: e.dma_start(
        out=STR[:], in_=STR_src[0].rearrange("t n -> (t n)").unsqueeze(0)),
        writes=[rSTR], dma=True, tag=pref + "_ldstr")
    S.op("sp", lambda e: e.dma_start(
        out=WG[:].rearrange("p (i e) -> p i e", i=NCH),
        in_=WGap.rearrange("(i p) e -> p i e", i=NCH)),
        writes=[rWG], dma=True, tag=pref + "_ldwg")
    bk = kb.ps_bank()
    for i in range(NCH):
        S.op("pe", lambda e, i=i, bk=bk: e.matmul(
            kb.ps[bk][0:1, 0:E0], kb.onesF[:, 0:1], WG[:, i * E0:(i + 1) * E0],
            start=(i == 0), stop=(i == NCH - 1)),
            reads=[rWG, "onesF"], writes=[f"psb{bk}"], tag=pref + f"_cs{i}")
    S.op("dve", lambda e, bk=bk: e.tensor_copy(CS[:], kb.ps[bk][0:1, 0:E0]),
         reads=[f"psb{bk}"], writes=[rCS], tag=pref + "_csev")
    if BGap is not None:
        BG = kb.sb(pref + "_bg", [1, E0], F32)
        S.op("sp", lambda e: e.dma_start(out=BG[:], in_=BGap),
             writes=[pref + "_bg"], dma=True, tag=pref + "_ldbg")

    XT = kb.sb(pref + "_xt", [128, C], F32)
    LG = kb.sb(pref + "_lg", [128, E0], F32)
    EXL = kb.sb(pref + "_exl", [128, E0], F32)
    TMP = kb.sb(pref + "_tmp", [128, E0], F32)
    MASK = kb.sb(pref + "_mask", [128, E0], F32)
    RT = kb.sb(pref + "_rt", [128, 4], F32)
    COMB = kb.sb(pref + "_comb", [128, E0], F32)
    rXT = pref + "_xt"
    rLG, rEXL, rTMP, rMASK, rRT, rCOMB = (pref + s for s in
                                          ("_lg", "_exl", "_tmp", "_mask", "_rt", "_comb"))
    for tt in range(NTT):
        S.op("sp", lambda e, tt=tt: e.dma_start(
            out=XT[:].rearrange("q (i n) -> q i n", i=NCH),
            in_=XSRC[:, :, tt * 128:(tt + 1) * 128].rearrange("i q n -> q i n")),
            writes=[rXT], dma=True, tag=pref + f"_ldx{tt}")
        bk = kb.ps_bank()
        for i in range(NCH):
            S.op("pe", lambda e, i=i, bk=bk: e.matmul(
                kb.ps[bk][:, 0:E0], XT[:, i * 128:(i + 1) * 128],
                WG[:, i * E0:(i + 1) * E0], start=(i == 0), stop=False),
                reads=[rXT, rWG], writes=[f"psb{bk}"], tag=pref + f"_lg{tt}_{i}")
        S.op("pe", lambda e, tt=tt, bk=bk: e.matmul(
            kb.ps[bk][:, 0:E0], STR[0:1, tt * 128:(tt + 1) * 128], CS[0:1, :],
            start=False, stop=True),
            reads=[rSTR, rCS], writes=[f"psb{bk}"], tag=pref + f"_nm{tt}")
        S.op("act", lambda e, tt=tt, bk=bk: e.activation(
            LG[:], kb.ps[bk][:, 0:E0], AF.Copy, scale=STT[:, 16 + tt:17 + tt]),
            reads=[f"psb{bk}", rSTT], writes=[rLG], tag=pref + f"_sc{tt}")
        if BGap is not None:
            bk2 = kb.ps_bank()
            S.op("pe", lambda e, bk2=bk2: e.matmul(
                kb.ps[bk2][:, 0:E0], kb.onesF[0:1, 0:128], BG[0:1, :],
                start=True, stop=True),
                reads=[pref + "_bg", "onesF"], writes=[f"psb{bk2}"],
                tag=pref + f"_bgb{tt}")
            S.op("dve", lambda e, bk2=bk2: e.tensor_add(
                LG[:], LG[:], kb.ps[bk2][:, 0:E0]),
                reads=[rLG, f"psb{bk2}"], writes=[rLG], tag=pref + f"_bga{tt}")
        S.op("dve", lambda e: e.reduce_max(RT[:, 0:1], LG[:], axis=AX.X, negate=True),
             reads=[rLG], writes=[rRT], tag=pref + f"_m1{tt}")
        S.op("act", lambda e: e.activation(EXL[:], LG[:], AF.Exp, bias=RT[:, 0:1]),
             reads=[rLG, rRT], writes=[rEXL], tag=pref + f"_ex{tt}")
        S.op("dve", lambda e: e.tensor_scalar(TMP[:], LG[:], RT[:, 0:1], None, ALU.add),
             reads=[rLG, rRT], writes=[rTMP], tag=pref + f"_d1{tt}")
        S.op("dve", lambda e: e.tensor_single_scalar(TMP[:], TMP[:], 0.0, ALU.is_ge),
             reads=[rTMP], writes=[rTMP], tag=pref + f"_is1{tt}")
        S.op("dve", lambda e: e.scalar_tensor_tensor(
            TMP[:], TMP[:], -1e30, LG[:], ALU.mult, ALU.add),
            reads=[rTMP, rLG], writes=[rTMP], tag=pref + f"_msk{tt}")
        S.op("dve", lambda e: e.reduce_max(RT[:, 1:2], TMP[:], axis=AX.X, negate=True),
             reads=[rTMP], writes=[rRT], tag=pref + f"_m2{tt}")
        S.op("dve", lambda e: e.tensor_scalar(MASK[:], LG[:], RT[:, 1:2], None, ALU.add),
             reads=[rLG, rRT], writes=[rMASK], tag=pref + f"_d2{tt}")
        S.op("dve", lambda e: e.tensor_single_scalar(MASK[:], MASK[:], 0.0, ALU.is_ge),
             reads=[rMASK], writes=[rMASK], tag=pref + f"_is2{tt}")
        S.op("dve", lambda e: e.tensor_mul(EXL[:], EXL[:], MASK[:]),
             reads=[rEXL, rMASK], writes=[rEXL], tag=pref + f"_cu{tt}")
        S.op("dve", lambda e: e.reduce_sum(RT[:, 2:3], EXL[:], axis=AX.X),
             reads=[rEXL], writes=[rRT], tag=pref + f"_ds{tt}")
        S.op("dve", lambda e: e.reciprocal(RT[:, 3:4], RT[:, 2:3]),
             reads=[rRT], writes=[rRT], tag=pref + f"_rc{tt}")
        S.op("dve", lambda e: e.tensor_scalar(COMB[:], EXL[:], RT[:, 3:4], None,
                                              ALU.mult),
             reads=[rEXL, rRT], writes=[rCOMB], tag=pref + f"_cw{tt}")
        bk3 = kb.ps_bank()
        S.op("pe", lambda e, bk3=bk3: e.transpose(
            kb.ps[bk3][0:E0, 0:128], COMB[:], kb.identF[:]),
            reads=[rCOMB, "identF"], writes=[f"psb{bk3}"], tag=pref + f"_ct{tt}")
        S.op("dve", lambda e, tt=tt, bk3=bk3: e.tensor_copy(
            COMBF[:, tt * 128:(tt + 1) * 128], kb.ps[bk3][0:E0, 0:128]),
            reads=[f"psb{bk3}"], writes=[rCF], tag=pref + f"_cev{tt}")
    CMB = kb.bufs.get(cmb_name)
    S.op("sp", lambda e: e.dma_start(out=CMB, in_=COMBF[:]),
         reads=[rCF], writes=[cmb_name], dma=True, tag=pref + "_cst")
    return CMB, cmb_name


def emit_moe(kb, pref, layer, XNsrc, CMB, rCMB, ACC, rACC):
    S = kb.S
    if layer == 0:
        W1ap, B1ap = kb.w["L0_ffn_W1"], kb.w["L0_ffn_b1"]
        W2ap, B2ap = kb.w["L0_ffn_W2"], kb.w["L0_ffn_b2"]
    else:
        W1ap, B1ap = kb.w["L1_ffn_w1"], kb.w["L1_ffn_b1"]
        W2ap, B2ap = kb.w["L1_ffn_w2"], kb.w["L1_ffn_b2"]
    NSL = 4
    HSL = 1024
    NJ = HSL // 128
    NAB = 2 if layer == 1 else 1

    S.op("dve", lambda e: e.memset(ACC[:], 0.0), writes=[rACC], tag=pref + "_accz")

    W1 = kb.sb(pref + "_w1", [128, NAB * NCH * HSL], BF16)
    W2 = kb.sb(pref + "_w2", [128, NJ * C], BF16)
    B1 = kb.sb(pref + "_b1", [1, NAB * HSL], BF16)
    B1F = kb.sb(pref + "_b1f", [1, NAB * HSL], F32)
    B2 = kb.sb(pref + "_b2", [1, C], BF16)
    B2F = kb.sb(pref + "_b2f", [1, C], F32)
    CE = kb.sb(pref + "_ce", [1, T], BF16)
    BC = kb.sb(pref + "_bc", [128, T], BF16)
    HB = kb.sb(pref + "_hb", [128, NJ * 512], BF16)
    XNT = [kb.sb(pref + f"_xnt{i}", [128, NCH * 512], BF16) for i in range(2)]
    if layer == 1:
        SA = kb.sb(pref + "_sa", [128, 512], F32)
        rSA = pref + "_sa"
    rW1, rW2, rB1, rB2, rBC, rHB = (pref + s for s in
                                    ("_w1", "_w2", "_b1", "_b2", "_bc", "_hb"))
    rB1F, rB2F, rCE = pref + "_b1f", pref + "_b2f", pref + "_ce"

    for ex in range(E0):
        S.op("pool", lambda e, ex=ex: e.dma_start(
            out=CE[:], in_=CMB[ex:ex + 1, :]),
            reads=[rCMB], writes=[rCE], dma=True, tag=pref + f"_ldce{ex}")
        for t4 in range(4):
            bk = kb.ps_bank()
            S.op("pe", lambda e, t4=t4, bk=bk: e.matmul(
                kb.ps[bk][:, 0:512], kb.onesB[0:1, 0:128],
                CE[0:1, t4 * 512:(t4 + 1) * 512], start=True, stop=True),
                reads=[rCE, "onesB"], writes=[f"psb{bk}"],
                tag=pref + f"_bcb{ex}_{t4}")
            S.op("act", lambda e, t4=t4, bk=bk: e.copy(
                BC[:, t4 * 512:(t4 + 1) * 512], kb.ps[bk][:, 0:512]),
                reads=[f"psb{bk}"], writes=[rBC], tag=pref + f"_bce{ex}_{t4}")
        S.op("sp", lambda e, ex=ex: e.dma_start(out=B2F[:], in_=B2ap[ex:ex + 1, :]),
             writes=[rB2F], dma=True, tag=pref + f"_ldb2{ex}")
        S.op("dve", lambda e: e.tensor_copy(B2[:], B2F[:]),
             reads=[rB2F], writes=[rB2], tag=pref + f"_cb2{ex}")
        for sl in range(NSL):
            for ab in range(NAB):
                c0 = (ab * FF1 if layer == 1 else 0) + sl * HSL
                srcw = W1ap[ex, :, c0:c0 + HSL]
                S.op("pool", lambda e, srcw=srcw, ab=ab: e.dma_start(
                    out=W1[:, ab * NCH * HSL:(ab + 1) * NCH * HSL].rearrange(
                        "p (i n) -> p i n", i=NCH),
                    in_=srcw.rearrange("(i p) n -> p i n", i=NCH)),
                    writes=[rW1], dma=True, tag=pref + f"_ldw1{ex}_{sl}_{ab}")
                S.op("sp", lambda e, ex=ex, c0=c0, ab=ab: e.dma_start(
                    out=B1F[0:1, ab * HSL:(ab + 1) * HSL],
                    in_=B1ap[ex:ex + 1, c0:c0 + HSL]),
                    writes=[rB1F], dma=True, tag=pref + f"_ldb1{ex}_{sl}_{ab}")
            S.op("dve", lambda e: e.tensor_copy(B1[:], B1F[:]),
                 reads=[rB1F], writes=[rB1], tag=pref + f"_cb1{ex}_{sl}")
            src2 = W2ap[ex, sl * HSL:(sl + 1) * HSL, :]
            S.op("pool", lambda e, src2=src2: e.dma_start(
                out=W2[:].rearrange("p (j n) -> p j n", j=NJ),
                in_=src2.rearrange("(j p) n -> p j n", j=NJ)),
                writes=[rW2], dma=True, tag=pref + f"_ldw2{ex}_{sl}")
            for t4 in range(4):
                xp = t4 % 2
                rXNT = pref + f"_xnt{xp}"
                S.op("sp", lambda e, t4=t4, xp=xp: e.dma_start(
                    out=XNT[xp][:].rearrange("p (i n) -> p i n", i=NCH),
                    in_=XNsrc[:, :, t4 * 512:(t4 + 1) * 512].rearrange(
                        "i p n -> p i n")),
                    writes=[rXNT], dma=True, tag=pref + f"_ldxn{ex}_{sl}_{t4}")
                for j in range(NJ):
                    if layer == 0:
                        bkh = 4 + (j % 2)
                        ps = kb.ps[bkh]
                        S.op("pe", lambda e, j=j, ps=ps: e.matmul(
                            ps[:, 0:512], B1[0:1, j * 128:(j + 1) * 128],
                            kb.onesB[0:1, 0:512], start=True, stop=False),
                            reads=[rB1, "onesB"], writes=[f"psb{bkh}"],
                            tag=pref + f"_hb{ex}_{sl}_{t4}_{j}")
                        for i in range(NCH):
                            S.op("pe", lambda e, i=i, j=j, xp=xp, ps=ps: e.matmul(
                                ps[:, 0:512],
                                W1[:, i * HSL + j * 128: i * HSL + j * 128 + 128],
                                XNT[xp][:, i * 512:(i + 1) * 512],
                                start=False, stop=(i == NCH - 1)),
                                reads=[rW1, rXNT], writes=[f"psb{bkh}"],
                                tag=pref + f"_h{ex}_{sl}_{t4}_{j}_{i}")
                        S.op("act", lambda e, j=j, ps=ps: e.activation(
                            HB[:, j * 512:(j + 1) * 512], ps[:, 0:512], AF.Gelu),
                            reads=[f"psb{bkh}"], writes=[rHB],
                            tag=pref + f"_gel{ex}_{sl}_{t4}_{j}")
                    else:
                        bka, bkb = 4 + (j % 2), 6
                        psa, psb_ = kb.ps[bka], kb.ps[bkb]
                        for ab, bk_, ps_ in ((0, bka, psa), (1, bkb, psb_)):
                            S.op("pe", lambda e, j=j, ab=ab, ps_=ps_: e.matmul(
                                ps_[:, 0:512],
                                B1[0:1, ab * HSL + j * 128: ab * HSL + j * 128 + 128],
                                kb.onesB[0:1, 0:512], start=True, stop=False),
                                reads=[rB1, "onesB"], writes=[f"psb{bk_}"],
                                tag=pref + f"_ub{ex}_{sl}_{t4}_{j}_{ab}")
                            for i in range(NCH):
                                S.op("pe", lambda e, i=i, j=j, xp=xp, ab=ab, ps_=ps_: e.matmul(
                                    ps_[:, 0:512],
                                    W1[:, ab * NCH * HSL + i * HSL + j * 128:
                                          ab * NCH * HSL + i * HSL + j * 128 + 128],
                                    XNT[xp][:, i * 512:(i + 1) * 512],
                                    start=False, stop=(i == NCH - 1)),
                                    reads=[rW1, rXNT], writes=[f"psb{bk_}"],
                                    tag=pref + f"_u{ex}_{sl}_{t4}_{j}_{ab}_{i}")
                        S.op("act", lambda e, psa=psa: e.activation(
                            SA[:], psa[:, 0:512], AF.Silu),
                            reads=[f"psb{bka}"], writes=[rSA],
                            tag=pref + f"_sil{ex}_{sl}_{t4}_{j}")
                        S.op("dve", lambda e, j=j, psb_=psb_: e.tensor_mul(
                            HB[:, j * 512:(j + 1) * 512], SA[:], psb_[:, 0:512]),
                            reads=[rSA, f"psb{bkb}"], writes=[rHB],
                            tag=pref + f"_gat{ex}_{sl}_{t4}_{j}")
                    S.op("dve", lambda e, j=j, t4=t4: e.tensor_mul(
                        HB[:, j * 512:(j + 1) * 512], HB[:, j * 512:(j + 1) * 512],
                        BC[:, t4 * 512:(t4 + 1) * 512]),
                        reads=[rHB, rBC], writes=[rHB],
                        tag=pref + f"_hs{ex}_{sl}_{t4}_{j}")
                for q2 in range(2):
                    t8 = 2 * t4 + q2
                    for f in range(NCH):
                        bko = f % 4
                        colo = (f // 4) * 256
                        pso = kb.ps[bko]
                        last = (sl == NSL - 1)
                        if last:
                            S.op("pe", lambda e, f=f, ex=ex, t8=t8, pso=pso, colo=colo: e.matmul(
                                pso[:, colo:colo + 256], B2[0:1, f * 128:(f + 1) * 128],
                                CE[0:1, t8 * 256:(t8 + 1) * 256],
                                start=True, stop=False),
                                reads=[rB2, rCE], writes=[f"psb{bko}"],
                                tag=pref + f"_b2m{ex}_{sl}_{t8}_{f}")
                        for j in range(NJ):
                            S.op("pe", lambda e, f=f, j=j, q2=q2, pso=pso, colo=colo,
                                 st=(not last and j == 0): e.matmul(
                                pso[:, colo:colo + 256],
                                W2[:, j * C + f * 128: j * C + f * 128 + 128],
                                HB[:, j * 512 + q2 * 256: j * 512 + q2 * 256 + 256],
                                start=st, stop=(j == NJ - 1)),
                                reads=[rW2, rHB], writes=[f"psb{bko}"],
                                tag=pref + f"_o{ex}_{sl}_{t8}_{f}_{j}")
                        S.op("dve", lambda e, f=f, t8=t8, pso=pso, colo=colo: e.scalar_tensor_tensor(
                            ACC[:, f * T + t8 * 256: f * T + (t8 + 1) * 256],
                            pso[:, colo:colo + 256], 1.0,
                            ACC[:, f * T + t8 * 256: f * T + (t8 + 1) * 256],
                            ALU.mult, ALU.add),
                            reads=[f"psb{bko}", rACC], writes=[rACC],
                            tag=pref + f"_mg{ex}_{sl}_{t8}_{f}")
    return (W1, rW1), (W2, rW2), (HB, rHB)


def phaseD(kb):
    S, nc = kb.S, kb.nc
    XFM = kb.bufs.get("XFM")
    XNF = kb.bufs.get("XNF")
    ST0T = kb.bufs.get("ST0T")
    ST0R = kb.bufs.get("ST0R")
    S0b = kb.bufs.get("S0")
    X1 = kb.bufs.get("X1")
    XN1 = kb.bufs.get("XN1")
    ST1T = kb.bufs.get("ST1T")
    ST1R = kb.bufs.get("ST1R")

    ACC = kb.sb("D_acc", [128, NCH * T], F32)
    CMB, rCMB = emit_routing(kb, "Dr", XFM, ST0T, ST0R,
                             kb.w["L0_ffn_Wg"], kb.w["L0_ffn_bg"], "CMB0")
    (W1, rW1), (W2, rW2), (HB, rHB) = emit_moe(
        kb, "Dm", 0, XNF, CMB, rCMB, ACC, "D_acc")

    if os.environ.get("MK_DBG"):
        FFN0 = kb.bufs.get("FFN0")
        for f in range(NCH):
            S.op("sp", lambda e, f=f: e.dma_start(
                out=FFN0[f], in_=ACC[:, f * T:(f + 1) * T]),
                reads=["D_acc"], writes=["FFN0"], dma=True, tag=f"D_dbga{f}")

    W1f = W1[:].bitcast(F32)
    BNM = W1f[:, 0:2048]
    BRS = W1f[:, 2048:4096]
    W2f = W2[:].bitcast(F32)
    ROW_A = W2f[0:1, 0:2048]
    ROW_B = W2f[0:1, 2048:4096]
    HBf = HB[:].bitcast(F32)
    ROW_C = HBf[0:1, 0:2048]
    F1 = kb.sb("D_f1", [128, T], F32)
    F2 = kb.sb("D_f2", [128, T], F32)
    F3 = kb.sb("D_f3", [128, T], F32)
    SSB = kb.sb("D_s", [128, T], BF16)
    GS = kb.sb("D_gs", [128, T], BF16)
    XN1SB = kb.sb("D_xn1", [128, T], BF16)
    STK = kb.sb("D_stk", [128, 32], F32)
    eps_ap = kb.kc(LN_EPS)
    mhalf_ap = kb.kc(-0.5)
    oneC_ap = kb.kc(1.0 / C)

    for f in range(NCH):
        S.op("sp", lambda e, f=f: e.dma_start(out=F1[:], in_=XFM[f]),
             writes=["D_f1"], dma=True, tag=f"D_ldx{f}")
        S.op("sp", lambda e, f=f: e.dma_start(out=SSB[:], in_=S0b[f]),
             writes=["D_s"], dma=True, tag=f"D_lds{f}")
        S.op("act", lambda e, f=f: e.activation(GS[:], ACC[:, f * T:(f + 1) * T],
                                                AF.Silu),
             reads=["D_acc"], writes=["D_gs"], tag=f"D_gs{f}")
        S.op("dve", lambda e: e.tensor_mul(F2[:], SSB[:], GS[:]),
             reads=["D_s", "D_gs"], writes=["D_f2"], tag=f"D_sg{f}")
        S.op("dve", lambda e: e.tensor_add(F3[:], F1[:], F2[:]),
             reads=["D_f1", "D_f2"], writes=["D_f3"], tag=f"D_x1{f}")
        S.op("sp", lambda e, f=f: e.dma_start(out=X1[f], in_=F3[:]),
             reads=["D_f3"], writes=["X1"], dma=True, tag=f"D_stx1{f}")
        S.op("act", lambda e: e.activation(F2[:], F3[:], AF.Square),
             reads=["D_f3"], writes=["D_f2"], tag=f"D_sq{f}")
        for t4 in range(4):
            S.op("pe", lambda e, f=f, t4=t4: e.matmul(
                kb.ps[t4][0:1, 0:512], kb.onesF[:, 0:1],
                F3[:, t4 * 512:(t4 + 1) * 512],
                start=(f == 0), stop=(f == NCH - 1)),
                reads=["D_f3", "onesF"], writes=[f"psb{t4}"], tag=f"D_sm{f}_{t4}")
            S.op("pe", lambda e, f=f, t4=t4: e.matmul(
                kb.ps[4 + t4][0:1, 0:512], kb.onesF[:, 0:1],
                F2[:, t4 * 512:(t4 + 1) * 512],
                start=(f == 0), stop=(f == NCH - 1)),
                reads=["D_f2", "onesF"], writes=[f"psb{4 + t4}"], tag=f"D_ssm{f}_{t4}")
    for t4 in range(4):
        S.op("dve", lambda e, t4=t4: e.tensor_copy(
            ROW_A[0:1, t4 * 512:(t4 + 1) * 512], kb.ps[t4][0:1, 0:512]),
            reads=[f"psb{t4}"], writes=[rW2], tag=f"D_rs{t4}")
        S.op("dve", lambda e, t4=t4: e.tensor_copy(
            ROW_B[0:1, t4 * 512:(t4 + 1) * 512], kb.ps[4 + t4][0:1, 0:512]),
            reads=[f"psb{4 + t4}"], writes=[rW2], tag=f"D_rss{t4}")
    S.op("act", lambda e: e.activation(ROW_C, ROW_A, AF.Square, scale=oneC_ap[0:1, :]),
         reads=[rW2, "kconst"], writes=[rHB], tag="D_m2")
    S.op("dve", lambda e: e.scalar_tensor_tensor(
        ROW_B, ROW_B, 1.0 / C, ROW_C, ALU.mult, ALU.subtract),
        reads=[rW2, rHB], writes=[rW2], tag="D_var")
    S.op("act", lambda e: e.activation(ROW_B, ROW_B, AF.Ln, bias=eps_ap[0:1, :]),
         reads=[rW2, "kconst"], writes=[rW2], tag="D_ln")
    S.op("act", lambda e: e.activation(ROW_B, ROW_B, AF.Exp, scale=mhalf_ap[0:1, :]),
         reads=[rW2, "kconst"], writes=[rW2], tag="D_rstd")
    S.op("dve", lambda e: e.tensor_scalar(ROW_A, ROW_A, -1.0 / C, None, ALU.mult),
         reads=[rW2], writes=[rW2], tag="D_nm")
    S.op("sp", lambda e: e.dma_start(
        out=ST1R[0].rearrange("t n -> (t n)").unsqueeze(0), in_=ROW_A),
        reads=[rW2], writes=["ST1R"], dma=True, tag="D_str0")
    S.op("sp", lambda e: e.dma_start(
        out=ST1R[1].rearrange("t n -> (t n)").unsqueeze(0), in_=ROW_B),
        reads=[rW2], writes=["ST1R"], dma=True, tag="D_str1")
    for s, rowap in ((0, ROW_A), (1, ROW_B)):
        for tt in range(NTT):
            bk = kb.ps_bank()
            S.op("pe", lambda e, rowap=rowap, tt=tt, bk=bk: e.transpose(
                kb.ps[bk][:, 0:1], rowap[0:1, tt * 128:(tt + 1) * 128],
                kb.identF[0:1, 0:1]),
                reads=[rW2, "identF"], writes=[f"psb{bk}"], tag=f"D_tk{s}_{tt}")
            S.op("dve", lambda e, s=s, tt=tt, bk=bk: e.tensor_copy(
                STK[:, s * 16 + tt:s * 16 + tt + 1], kb.ps[bk][:, 0:1]),
                reads=[f"psb{bk}"], writes=["D_stk"], tag=f"D_tke{s}_{tt}")
    S.op("sp", lambda e: e.dma_start(
        out=ST1T.rearrange("s p n -> p s n"),
        in_=STK[:].rearrange("p (s n) -> p s n", s=2)),
        reads=["D_stk"], writes=["ST1T"], dma=True, tag="D_stt")
    for t4 in range(4):
        for (rowap, dst) in ((ROW_A, BNM), (ROW_B, BRS)):
            bk = kb.ps_bank()
            S.op("pe", lambda e, rowap=rowap, t4=t4, bk=bk: e.matmul(
                kb.ps[bk][:, 0:512], kb.onesF[0:1, 0:128],
                rowap[0:1, t4 * 512:(t4 + 1) * 512],
                start=True, stop=True),
                reads=[rW2, "onesF"], writes=[f"psb{bk}"], tag=f"D_bc{t4}_{id(dst)}")
            S.op("dve", lambda e, dst=dst, t4=t4, bk=bk: e.tensor_copy(
                dst[:, t4 * 512:(t4 + 1) * 512], kb.ps[bk][:, 0:512]),
                reads=[f"psb{bk}"], writes=[rW1], tag=f"D_bce{t4}_{id(dst)}")
    for f in range(NCH):
        S.op("sp", lambda e, f=f: e.dma_start(out=F1[:], in_=X1[f]),
             reads=["X1"], writes=["D_f1"], dma=True, tag=f"D_rlx{f}")
        S.op("dve", lambda e: e.tensor_add(F2[:], F1[:], BNM),
             reads=["D_f1", rW1], writes=["D_f2"], tag=f"D_xa{f}")
        S.op("dve", lambda e: e.tensor_mul(XN1SB[:], F2[:], BRS),
             reads=["D_f2", rW1], writes=["D_xn1"], tag=f"D_xm{f}")
        S.op("sp", lambda e, f=f: e.dma_start(out=XN1[f], in_=XN1SB[:]),
             reads=["D_xn1"], writes=["XN1"], dma=True, tag=f"D_stxn{f}")


def phaseE(kb):
    S, nc = kb.S, kb.nc
    X1 = kb.bufs.get("X1")
    XN1 = kb.bufs.get("XN1")
    ST1T = kb.bufs.get("ST1T")
    ST1R = kb.bufs.get("ST1R")
    G1 = kb.bufs.get("G1")

    ACC = kb.sb("E_acc", [128, NCH * T], F32)
    CMB, rCMB = emit_routing(kb, "Er", X1, ST1T, ST1R, kb.w["L1_ffn_wrT"], None, "CMB1")
    emit_moe(kb, "Em", 1, XN1, CMB, rCMB, ACC, "E_acc")

    GS = kb.sb("E_gs", [128, T], BF16)
    for f in range(NCH):
        S.op("act", lambda e, f=f: e.activation(GS[:], ACC[:, f * T:(f + 1) * T],
                                                AF.Silu),
             reads=["E_acc"], writes=["E_gs"], tag=f"E_gs{f}")
        S.op("sp", lambda e, f=f: e.dma_start(out=G1[f], in_=GS[:]),
             reads=["E_gs"], writes=["G1"], dma=True, tag=f"E_stg{f}")


def phaseF(kb):
    S, nc = kb.S, kb.nc
    XN1 = kb.bufs.get("XN1")
    PR = {m: kb.bufs.get("PR" + m) for m in "rwkvag"}

    XN = kb.sb("F_xn", [128, NCH * T], BF16)
    S.op("sp", lambda e: e.dma_start(
        out=XN[:].rearrange("p (i t) -> p i t", i=NCH),
        in_=XN1.rearrange("i p t -> p i t")),
        writes=["F_xn"], dma=True, tag="F_ldxn")
    MLIST = ["r", "w", "k", "v", "a", "g"]
    for i in range(NCH):
        for mi, m in enumerate(MLIST):
            S.op("dve", lambda e, i=i, mi=mi, m=m: e.tensor_scalar(
                kb.vcol[i][:, 20 + mi:21 + mi], kb.vc("mu_" + m, i), -1.0, 1.0,
                ALU.mult, ALU.add),
                reads=[f"vcol{i}"], writes=[f"vcol{i}"], tag=f"F_om{i}_{m}")

    WM = [kb.sb(f"F_w{i}", [128, NCH * C], BF16) for i in range(2)]
    MX = kb.sb("F_mx", [128, NCH * 256], BF16)
    T1 = kb.sb("F_t1", [128, 256], BF16)
    UF = kb.sb("F_uf", [128, 256], F32)
    AB = kb.sb("F_ab", [128, 256], F32)
    LT = kb.sb("F_lt", [128, 256], F32)
    MN = kb.sb("F_mn", [128, 256], F32)
    STG = kb.sb("F_stg", [128, NCH * 256], BF16)
    one_ap = kb.kc(1.0)
    WNAME = {"r": "L1_attn_Wr", "w": "L1_attn_Ww", "k": "L1_attn_Wk",
             "v": "L1_attn_Wv", "a": "L1_attn_Wa", "g": "L1_attn_Wg"}

    for mi, m in enumerate(MLIST):
        wp = mi % 2
        rWM = f"F_w{wp}"
        wap = kb.w[WNAME[m]]
        S.op("pool", lambda e, wp=wp, wap=wap: e.dma_start(
            out=WM[wp][:].rearrange("p (i f) -> p i f", i=NCH),
            in_=wap.rearrange("(i p) f -> p i f", i=NCH)),
            writes=[rWM], dma=True, tag=f"F_ldw{m}")
        for b in range(B):
            for i in range(NCH):
                cur = XN[:, i * T + b * 256: i * T + (b + 1) * 256]
                if b == 0:
                    S.op("dve", lambda e, i=i, cur=cur, mi=mi: e.tensor_scalar(
                        MX[:, i * 256:(i + 1) * 256], cur,
                        kb.vcol[i][:, 20 + mi:21 + mi], None, ALU.mult),
                        reads=["F_xn", f"vcol{i}"], writes=["F_mx"],
                        tag=f"F_mx{m}_{b}_{i}")
                else:
                    prv = XN[:, i * T + (b - 1) * 256: i * T + b * 256]
                    S.op("dve", lambda e, i=i, cur=cur, mi=mi: e.tensor_scalar(
                        T1[:], cur, kb.vcol[i][:, 20 + mi:21 + mi], None, ALU.mult),
                        reads=["F_xn", f"vcol{i}"], writes=["F_t1"],
                        tag=f"F_mxa{m}_{b}_{i}")
                    S.op("dve", lambda e, i=i, prv=prv, m=m: e.scalar_tensor_tensor(
                        MX[:, i * 256:(i + 1) * 256], prv, kb.vc("mu_" + m, i),
                        T1[:], ALU.mult, ALU.add),
                        reads=["F_xn", f"vcol{i}", "F_t1"], writes=["F_mx"],
                        tag=f"F_mxb{m}_{b}_{i}")
            for f in range(NCH):
                bk = kb.ps_bank()
                ps = kb.ps[bk]
                for i in range(NCH):
                    S.op("pe", lambda e, wp=wp, i=i, f=f, ps=ps: e.matmul(
                        ps[:, 0:256], WM[wp][:, i * C + f * 128: i * C + f * 128 + 128],
                        MX[:, i * 256:(i + 1) * 256],
                        start=(i == 0), stop=(i == NCH - 1)),
                        reads=[rWM, "F_mx"], writes=[f"psb{bk}"],
                        tag=f"F_p{m}_{b}_{f}_{i}")
                dst = STG[:, f * 256:(f + 1) * 256]
                if m == "r" or m == "k" or m == "v":
                    S.op("act", lambda e, dst=dst, ps=ps: e.copy(dst, ps[:, 0:256]),
                         reads=[f"psb{bk}"], writes=["F_stg"], tag=f"F_e{m}_{b}_{f}")
                elif m == "a":
                    S.op("act", lambda e, dst=dst, ps=ps, f=f: e.activation(
                        dst, ps[:, 0:256], AF.Sigmoid, bias=kb.vc("a0", f)),
                        reads=[f"psb{bk}", f"vcol{f}"], writes=["F_stg"],
                        tag=f"F_e{m}_{b}_{f}")
                elif m == "g":
                    S.op("act", lambda e, dst=dst, ps=ps: e.activation(
                        dst, ps[:, 0:256], AF.Sigmoid),
                        reads=[f"psb{bk}"], writes=["F_stg"], tag=f"F_e{m}_{b}_{f}")
                else:
                    S.op("dve", lambda e, ps=ps, f=f: e.tensor_scalar(
                        UF[:], ps[:, 0:256], kb.vc("w0", f), None, ALU.add),
                        reads=[f"psb{bk}", f"vcol{f}"], writes=["F_uf"],
                        tag=f"F_wu{b}_{f}")
                    S.op("act", lambda e: e.activation(AB[:], UF[:], AF.Abs),
                         reads=["F_uf"], writes=["F_ab"], tag=f"F_wab{b}_{f}")
                    S.op("act", lambda e: e.activation(AB[:], AB[:], AF.Exp,
                                                       scale=-1.0),
                         reads=["F_ab"], writes=["F_ab"], tag=f"F_we{b}_{f}")
                    S.op("act", lambda e: e.activation(LT[:], AB[:], AF.Ln,
                                                       bias=one_ap),
                         reads=["F_ab", "kconst"], writes=["F_lt"], tag=f"F_wl{b}_{f}")
                    S.op("dve", lambda e: e.tensor_scalar(
                        MN[:], UF[:], 0.0, None, ALU.min),
                        reads=["F_uf"], writes=["F_mn"], tag=f"F_wm{b}_{f}")
                    S.op("dve", lambda e: e.scalar_tensor_tensor(
                        MN[:], MN[:], -0.5, LT[:], ALU.add, ALU.subtract),
                        reads=["F_mn", "F_lt"], writes=["F_mn"], tag=f"F_ws{b}_{f}")
                    S.op("act", lambda e: e.activation(MN[:], MN[:], AF.Exp),
                         reads=["F_mn"], writes=["F_mn"], tag=f"F_we2{b}_{f}")
                    S.op("act", lambda e, dst=dst: e.activation(
                        dst, MN[:], AF.Exp, scale=-1.0),
                        reads=["F_mn"], writes=["F_stg"], tag=f"F_we3{b}_{f}")
            S.op("sp", lambda e, m=m, b=b: e.dma_start(
                out=PR[m][:, :, b * 256:(b + 1) * 256].rearrange("i p n -> p i n"),
                in_=STG[:].rearrange("p (i n) -> p i n", i=NCH)),
                reads=["F_stg"], writes=["PR" + m], dma=True, tag=f"F_st{m}_{b}")


def phaseG(kb):
    S, nc = kb.S, kb.nc
    XN1 = kb.bufs.get("XN1")
    X1 = kb.bufs.get("X1")
    G1 = kb.bufs.get("G1")
    PR = {m: kb.bufs.get("PR" + m) for m in "rwkvag"}
    OUT = kb.bufs.get("OUT")

    WO = kb.sb("G_wo", [128, NCH * C], BF16)
    WF = kb.sb("G_wf", [128, NCH * C], BF16)
    S.op("pool", lambda e: e.dma_start(
        out=WO[:].rearrange("p (i f) -> p i f", i=NCH),
        in_=kb.w["L1_attn_Wo"].rearrange("(i p) f -> p i f", i=NCH)),
        writes=["G_wo"], dma=True, tag="G_ldwo")
    S.op("pool", lambda e: e.dma_start(
        out=WF[:].rearrange("p (i f) -> p i f", i=NCH),
        in_=kb.w["L1_attn_Wf"].rearrange("(i p) f -> p i f", i=NCH)),
        writes=["G_wf"], dma=True, tag="G_ldwf")

    VST = kb.sb("G_vst", [128, B * T], BF16)
    Q = kb.sb("G_q", [128, B * T], BF16)
    R = kb.sb("G_r", [128, T], BF16)
    K_ = kb.sb("G_k", [128, T], BF16)
    AT_ = kb.sb("G_a", [128, T], BF16)
    GT = kb.sb("G_g", [128, T], BF16)
    WD = kb.sb("G_wd", [128, T], BF16)
    KKN = kb.sb("G_kkn", [128, T], BF16)
    ZP = kb.sb("G_zp", [128, T], BF16)
    KM = kb.sb("G_km", [128, T], BF16)
    Y = kb.sb("G_y", [128, T], F32)
    YG = kb.sb("G_yg", [128, T], BF16)
    TMB = kb.sb("G_tmb", [128, 256], BF16)
    TM2 = kb.sb("G_tm2", [128, 256], F32)
    TM3 = kb.sb("G_tm3", [128, 256], F32)
    SS = kb.sb("G_ss", [128, 256], F32)
    XNT = kb.sb("G_xnt", [128, T], BF16)
    ATL = kb.sb("G_atl", [128, T], F32)
    HT = kb.sb("G_ht", [128, T], BF16)
    SV = kb.sb("G_sv", [128, T], F32)
    STAU = kb.sb("G_stau", [128, T], BF16)
    G1T = kb.sb("G_g1t", [128, T], BF16)
    OT = ATL
    XT1 = SV
    OW = kb.sb("G_ow", [128, C], F32)
    ROW = kb.sb("G_row", [1, 6 * 256], F32)
    BNM = kb.sb("G_bnm", [128, 256], F32)
    BRS = kb.sb("G_brs", [128, 256], F32)
    SQT = kb.sb("G_sqt", [128, 256], F32)
    eps_ap = kb.kc(LN_EPS)
    gn_ap = kb.kc(GN_EPS)
    mhalf_ap = kb.kc(-0.5)
    tiny_ap = kb.kc(1e-12)
    o64_ap = kb.kc(1.0 / 64.0)
    oneC_ap = kb.kc(1.0 / C)

    def ln_tail(tau, SRC, rSRC, DST, rDST, wname, bname, eps_bias):
        for i in range(NCH):
            S.op("pe", lambda e, i=i: e.matmul(
                kb.ps[0][0:1, 0:256], kb.onesF[:, 0:1],
                SRC[:, i * 256:(i + 1) * 256],
                start=(i == 0), stop=(i == NCH - 1)),
                reads=[rSRC, "onesF"], writes=["psb0"], tag=f"G_ts{tau}_{i}{wname}")
            S.op("act", lambda e, i=i: e.activation(
                SQT[:], SRC[:, i * 256:(i + 1) * 256], AF.Square),
                reads=[rSRC], writes=["G_sqt"], tag=f"G_tq{tau}_{i}{wname}")
            S.op("pe", lambda e, i=i: e.matmul(
                kb.ps[1][0:1, 0:256], kb.onesF[:, 0:1], SQT[:],
                start=(i == 0), stop=(i == NCH - 1)),
                reads=["G_sqt", "onesF"], writes=["psb1"], tag=f"G_tss{tau}_{i}{wname}")
        S.op("dve", lambda e: e.tensor_copy(ROW[0:1, 0:256], kb.ps[0][0:1, 0:256]),
             reads=["psb0"], writes=["G_row"], tag=f"G_tr{tau}{wname}")
        S.op("dve", lambda e: e.tensor_copy(ROW[0:1, 256:512], kb.ps[1][0:1, 0:256]),
             reads=["psb1"], writes=["G_row"], tag=f"G_tr2{tau}{wname}")
        S.op("dve", lambda e: e.tensor_scalar(
            ROW[0:1, 512:768], ROW[0:1, 0:256], -1.0 / C, None, ALU.mult),
            reads=["G_row"], writes=["G_row"], tag=f"G_tnm{tau}{wname}")
        S.op("act", lambda e: e.activation(
            ROW[0:1, 1024:1280], ROW[0:1, 0:256], AF.Square,
            scale=oneC_ap[0:1, :]),
            reads=["G_row", "kconst"], writes=["G_row"], tag=f"G_tm2{tau}{wname}")
        S.op("dve", lambda e: e.scalar_tensor_tensor(
            ROW[0:1, 768:1024], ROW[0:1, 256:512], 1.0 / C, ROW[0:1, 1024:1280],
            ALU.mult, ALU.subtract),
            reads=["G_row"], writes=["G_row"], tag=f"G_tv{tau}{wname}")
        S.op("act", lambda e: e.activation(
            ROW[0:1, 768:1024], ROW[0:1, 768:1024], AF.Ln, bias=eps_bias[0:1, :]),
            reads=["G_row", "kconst"], writes=["G_row"], tag=f"G_tl{tau}{wname}")
        S.op("act", lambda e: e.activation(
            ROW[0:1, 768:1024], ROW[0:1, 768:1024], AF.Exp, scale=mhalf_ap[0:1, :]),
            reads=["G_row", "kconst"], writes=["G_row"], tag=f"G_te{tau}{wname}")
        for (off, dst, rn) in ((512, BNM, "G_bnm"), (768, BRS, "G_brs")):
            bk = kb.ps_bank()
            S.op("pe", lambda e, off=off, bk=bk: e.matmul(
                kb.ps[bk][:, 0:256], kb.onesF[0:1, 0:128], ROW[0:1, off:off + 256],
                start=True, stop=True),
                reads=["G_row", "onesF"], writes=[f"psb{bk}"],
                tag=f"G_tb{tau}_{off}{wname}")
            S.op("dve", lambda e, dst=dst, bk=bk: e.tensor_copy(
                dst[:], kb.ps[bk][:, 0:256]),
                reads=[f"psb{bk}"], writes=[rn], tag=f"G_tbe{tau}_{off}{wname}")
        for i in range(NCH):
            S.op("dve", lambda e, i=i: e.tensor_add(
                TM2[:], SRC[:, i * 256:(i + 1) * 256], BNM[:]),
                reads=[rSRC, "G_bnm"], writes=["G_tm2"], tag=f"G_ta{tau}_{i}{wname}")
            S.op("dve", lambda e, i=i: e.tensor_mul(TM3[:], TM2[:], BRS[:]),
                 reads=["G_tm2", "G_brs"], writes=["G_tm3"], tag=f"G_tm{tau}_{i}{wname}")
            S.op("dve", lambda e, i=i: e.tensor_scalar(
                DST[:, i * 256:(i + 1) * 256], TM3[:], kb.vc(wname, i),
                kb.vc(bname, i), ALU.mult, ALU.add),
                reads=["G_tm3", f"vcol{i}"], writes=[rDST], tag=f"G_tf{tau}_{i}{wname}")

    for i in range(NCH):
        S.op("dve", lambda e, i=i: e.tensor_scalar(
            kb.vcol[i][:, 26:27], kb.vc("k_a", i), -1.0, 1.0, ALU.mult, ALU.add),
            reads=[f"vcol{i}"], writes=[f"vcol{i}"], tag=f"G_omka{i}")

    for tau in range(B):
        cols = slice(tau * 256, (tau + 1) * 256)
        for (m, dst, rn) in (("r", R, "G_r"), ("k", K_, "G_k"), ("a", AT_, "G_a"),
                             ("g", GT, "G_g"), ("w", WD, "G_wd")):
            S.op("sp", lambda e, m=m, dst=dst, cols=cols: e.dma_start(
                out=dst[:].rearrange("p (i n) -> p i n", i=NCH),
                in_=PR[m][:, :, cols].rearrange("i p n -> p i n")),
                writes=[rn], dma=True, tag=f"G_ld{m}{tau}")
        S.op("sp", lambda e, tau=tau, cols=cols: e.dma_start(
            out=VST[:, tau * T:(tau + 1) * T].rearrange("p (i n) -> p i n", i=NCH),
            in_=PR["v"][:, :, cols].rearrange("i p n -> p i n")),
            writes=["G_vst"], dma=True, tag=f"G_ldv{tau}")
        S.op("sp", lambda e, cols=cols: e.dma_start(
            out=XNT[:].rearrange("p (i n) -> p i n", i=NCH),
            in_=XN1[:, :, cols].rearrange("i p n -> p i n")),
            writes=["G_xnt"], dma=True, tag=f"G_ldxn{tau}")
        for i in range(NCH):
            cs = slice(i * 256, (i + 1) * 256)
            S.op("dve", lambda e, cs=cs, i=i: e.tensor_scalar(
                KKN[:, cs], K_[:, cs], kb.vc("k_k", i), None, ALU.mult),
                reads=["G_k", f"vcol{i}"], writes=["G_kkn"], tag=f"G_kk{tau}_{i}")
            S.op("act", lambda e, cs=cs: e.activation(TMB[:], KKN[:, cs], AF.Square),
                 reads=["G_kkn"], writes=["G_tmb"], tag=f"G_kk2{tau}_{i}")
            bk = kb.ps_bank()
            S.op("pe", lambda e, bk=bk: e.matmul(
                kb.ps[bk][:, 0:256], kb.b64[:], TMB[:], start=True, stop=True),
                reads=["G_tmb", "b64"], writes=[f"psb{bk}"], tag=f"G_kkm{tau}_{i}")
            S.op("act", lambda e, bk=bk: e.activation(SS[:], kb.ps[bk][:, 0:256],
                                                      AF.Sqrt),
                 reads=[f"psb{bk}"], writes=["G_ss"], tag=f"G_kks{tau}_{i}")
            S.op("dve", lambda e: e.tensor_scalar(SS[:], SS[:], 1e-12, None, ALU.max),
                 reads=["G_ss"], writes=["G_ss"], tag=f"G_kkx{tau}_{i}")
            S.op("dve", lambda e: e.reciprocal(TM2[:], SS[:]),
                 reads=["G_ss"], writes=["G_tm2"], tag=f"G_kkr{tau}_{i}")
            S.op("dve", lambda e, cs=cs: e.tensor_mul(KKN[:, cs], KKN[:, cs], TM2[:]),
                 reads=["G_kkn", "G_tm2"], writes=["G_kkn"], tag=f"G_kkn{tau}_{i}")
            S.op("dve", lambda e, cs=cs: e.scalar_tensor_tensor(
                ZP[:, cs], KKN[:, cs], -1.0, AT_[:, cs], ALU.mult, ALU.mult),
                reads=["G_kkn", "G_a"], writes=["G_zp"], tag=f"G_zp{tau}_{i}")
            S.op("dve", lambda e, cs=cs, i=i: e.tensor_scalar(
                TM2[:], AT_[:, cs], kb.vc("k_a", i), kb.vcol[i][:, 26:27],
                ALU.mult, ALU.add),
                reads=["G_a", f"vcol{i}"], writes=["G_tm2"], tag=f"G_ka{tau}_{i}")
            S.op("dve", lambda e, cs=cs: e.tensor_mul(KM[:, cs], K_[:, cs], TM2[:]),
                 reads=["G_k", "G_tm2"], writes=["G_km"], tag=f"G_km{tau}_{i}")
        S.op("dve", lambda e, tau=tau: e.tensor_copy(Q[:, tau * T:(tau + 1) * T],
                                                     KM[:]),
             reads=["G_km"], writes=["G_q"], tag=f"G_qi{tau}")
        for s_ in range(tau):
            qs = slice(s_ * T, (s_ + 1) * T)
            for i in range(NCH):
                cs = slice(i * 256, (i + 1) * 256)
                qsi = slice(s_ * T + i * 256, s_ * T + (i + 1) * 256)
                S.op("dve", lambda e, qsi=qsi, cs=cs: e.tensor_mul(
                    TMB[:], Q[:, qsi], KKN[:, cs]),
                    reads=["G_q", "G_kkn"], writes=["G_tmb"], tag=f"G_pd{tau}_{s_}_{i}")
                bk = kb.ps_bank()
                S.op("pe", lambda e, bk=bk: e.matmul(
                    kb.ps[bk][:, 0:256], kb.b64[:], TMB[:], start=True, stop=True),
                    reads=["G_tmb", "b64"], writes=[f"psb{bk}"],
                    tag=f"G_pm{tau}_{s_}_{i}")
                S.op("dve", lambda e, qsi=qsi, cs=cs: e.tensor_mul(
                    TM2[:], Q[:, qsi], WD[:, cs]),
                    reads=["G_q", "G_wd"], writes=["G_tm2"], tag=f"G_pw{tau}_{s_}_{i}")
                S.op("dve", lambda e, bk=bk, cs=cs: e.tensor_mul(
                    TM3[:], kb.ps[bk][:, 0:256], ZP[:, cs]),
                    reads=[f"psb{bk}", "G_zp"], writes=["G_tm3"],
                    tag=f"G_pz{tau}_{s_}_{i}")
                S.op("dve", lambda e, qsi=qsi: e.tensor_add(
                    Q[:, qsi], TM2[:], TM3[:]),
                    reads=["G_tm2", "G_tm3"], writes=["G_q"], tag=f"G_pq{tau}_{s_}_{i}")
        S.op("dve", lambda e: e.memset(Y[:], 0.0), writes=["G_y"], tag=f"G_yz{tau}")
        for s_ in range(tau + 1):
            for i in range(NCH):
                cs = slice(i * 256, (i + 1) * 256)
                qsi = slice(s_ * T + i * 256, s_ * T + (i + 1) * 256)
                vsi = slice(s_ * T + i * 256, s_ * T + (i + 1) * 256)
                S.op("dve", lambda e, qsi=qsi, cs=cs: e.tensor_mul(
                    TMB[:], Q[:, qsi], R[:, cs]),
                    reads=["G_q", "G_r"], writes=["G_tmb"], tag=f"G_yd{tau}_{s_}_{i}")
                bk = kb.ps_bank()
                S.op("pe", lambda e, bk=bk: e.matmul(
                    kb.ps[bk][:, 0:256], kb.b64[:], TMB[:], start=True, stop=True),
                    reads=["G_tmb", "b64"], writes=[f"psb{bk}"],
                    tag=f"G_ym{tau}_{s_}_{i}")
                S.op("dve", lambda e, bk=bk, vsi=vsi: e.tensor_mul(
                    TM2[:], kb.ps[bk][:, 0:256], VST[:, vsi]),
                    reads=[f"psb{bk}", "G_vst"], writes=["G_tm2"],
                    tag=f"G_yv{tau}_{s_}_{i}")
                S.op("dve", lambda e, cs=cs: e.tensor_add(
                    Y[:, cs], Y[:, cs], TM2[:]),
                    reads=["G_tm2", "G_y"], writes=["G_y"], tag=f"G_ya{tau}_{s_}_{i}")
        if os.environ.get("MK_DBG"):
            DBY = kb.bufs.get("DBG_Y")
            S.op("sp", lambda e, tau=tau: e.dma_start(
                out=DBY[tau], in_=Y[:]),
                reads=["G_y"], writes=["DBG_Y"], dma=True, tag=f"G_dby{tau}")
        for i in range(NCH):
            cs = slice(i * 256, (i + 1) * 256)
            S.op("act", lambda e, cs=cs: e.activation(TMB[:], Y[:, cs], AF.Square),
                 reads=["G_y"], writes=["G_tmb"], tag=f"G_n2{tau}_{i}")
            bk = kb.ps_bank()
            S.op("pe", lambda e, bk=bk: e.matmul(
                kb.ps[bk][:, 0:256], kb.b64[:], TMB[:], start=True, stop=True),
                reads=["G_tmb", "b64"], writes=[f"psb{bk}"], tag=f"G_nm2{tau}_{i}")
            S.op("act", lambda e, cs=cs: e.copy(TMB[:], Y[:, cs]),
                 reads=["G_y"], writes=["G_tmb"], tag=f"G_nc{tau}_{i}")
            bk2 = kb.ps_bank()
            S.op("pe", lambda e, bk2=bk2: e.matmul(
                kb.ps[bk2][:, 0:256], kb.b64[:], TMB[:], start=True, stop=True),
                reads=["G_tmb", "b64"], writes=[f"psb{bk2}"], tag=f"G_nm1{tau}_{i}")
            S.op("act", lambda e, bk2=bk2: e.activation(
                TM2[:], kb.ps[bk2][:, 0:256], AF.Copy, scale=o64_ap),
                reads=[f"psb{bk2}", "kconst"], writes=["G_tm2"], tag=f"G_nmn{tau}_{i}")
            S.op("act", lambda e: e.activation(TM3[:], TM2[:], AF.Square),
                 reads=["G_tm2"], writes=["G_tm3"], tag=f"G_nm22{tau}_{i}")
            S.op("dve", lambda e, bk=bk: e.scalar_tensor_tensor(
                TM3[:], kb.ps[bk][:, 0:256], 1.0 / 64.0, TM3[:],
                ALU.mult, ALU.subtract),
                reads=[f"psb{bk}", "G_tm3"], writes=["G_tm3"], tag=f"G_nv{tau}_{i}")
            S.op("act", lambda e: e.activation(TM3[:], TM3[:], AF.Ln, bias=gn_ap),
                 reads=["G_tm3", "kconst"], writes=["G_tm3"], tag=f"G_nl{tau}_{i}")
            S.op("act", lambda e: e.activation(TM3[:], TM3[:], AF.Exp, scale=mhalf_ap),
                 reads=["G_tm3", "kconst"], writes=["G_tm3"], tag=f"G_ne{tau}_{i}")
            S.op("dve", lambda e, cs=cs: e.tensor_sub(SS[:], Y[:, cs], TM2[:]),
                 reads=["G_y", "G_tm2"], writes=["G_ss"], tag=f"G_ny{tau}_{i}")
            S.op("dve", lambda e: e.tensor_mul(SS[:], SS[:], TM3[:]),
                 reads=["G_ss", "G_tm3"], writes=["G_ss"], tag=f"G_ny2{tau}_{i}")
            S.op("dve", lambda e, i=i: e.tensor_scalar(
                SS[:], SS[:], kb.vc("lnx_w", i), kb.vc("lnx_b", i),
                ALU.mult, ALU.add),
                reads=["G_ss", f"vcol{i}"], writes=["G_ss"], tag=f"G_nf{tau}_{i}")
            S.op("dve", lambda e, cs=cs, i=i: e.tensor_scalar(
                TM2[:], R[:, cs], kb.vc("r_k", i), None, ALU.mult),
                reads=["G_r", f"vcol{i}"], writes=["G_tm2"], tag=f"G_rk{tau}_{i}")
            S.op("dve", lambda e, cs=cs: e.tensor_mul(TMB[:], TM2[:], KM[:, cs]),
                 reads=["G_tm2", "G_km"], writes=["G_tmb"], tag=f"G_rk2{tau}_{i}")
            bk3 = kb.ps_bank()
            S.op("pe", lambda e, bk3=bk3: e.matmul(
                kb.ps[bk3][:, 0:256], kb.b64[:], TMB[:], start=True, stop=True),
                reads=["G_tmb", "b64"], writes=[f"psb{bk3}"], tag=f"G_rkm{tau}_{i}")
            S.op("dve", lambda e, bk3=bk3, tau=tau, i=i: e.tensor_mul(
                TM2[:], kb.ps[bk3][:, 0:256],
                VST[:, tau * T + i * 256: tau * T + (i + 1) * 256]),
                reads=[f"psb{bk3}", "G_vst"], writes=["G_tm2"], tag=f"G_rkv{tau}_{i}")
            S.op("dve", lambda e: e.tensor_add(SS[:], SS[:], TM2[:]),
                 reads=["G_ss", "G_tm2"], writes=["G_ss"], tag=f"G_rka{tau}_{i}")
            S.op("dve", lambda e, cs=cs: e.tensor_mul(YG[:, cs], SS[:], GT[:, cs]),
                 reads=["G_ss", "G_g"], writes=["G_yg"], tag=f"G_gy{tau}_{i}")
        if os.environ.get("MK_DBG"):
            DBYG = kb.bufs.get("DBG_YG")
            S.op("pool", lambda e, tau=tau: e.dma_start(
                out=DBYG[tau], in_=YG[:]),
                reads=["G_yg"], writes=["DBG_YG"], dma=True, tag=f"G_dbyg{tau}")
        for f in range(NCH):
            bk = kb.ps_bank()
            ps = kb.ps[bk]
            for i in range(NCH):
                S.op("pe", lambda e, i=i, f=f, ps=ps: e.matmul(
                    ps[:, 0:256], WO[:, i * C + f * 128: i * C + f * 128 + 128],
                    YG[:, i * 256:(i + 1) * 256],
                    start=(i == 0), stop=(i == NCH - 1)),
                    reads=["G_wo", "G_yg"], writes=[f"psb{bk}"],
                    tag=f"G_wo{tau}_{f}_{i}")
            S.op("dve", lambda e, f=f, ps=ps: e.tensor_add(
                ATL[:, f * 256:(f + 1) * 256], ps[:, 0:256],
                XNT[:, f * 256:(f + 1) * 256]),
                reads=[f"psb{bk}", "G_xnt"], writes=["G_atl"], tag=f"G_at{tau}_{f}")
        if os.environ.get("MK_DBG"):
            DBAT = kb.bufs.get("DBG_AT")
            S.op("sp", lambda e, tau=tau: e.dma_start(
                out=DBAT[tau], in_=ATL[:]),
                reads=["G_atl"], writes=["DBG_AT"], dma=True, tag=f"G_dbat{tau}")
        ln_tail(tau, ATL, "G_atl", HT, "G_ht", "norm_w", "norm_b", eps_ap)
        for f in range(NCH):
            bk = kb.ps_bank()
            ps = kb.ps[bk]
            for i in range(NCH):
                S.op("pe", lambda e, i=i, f=f, ps=ps: e.matmul(
                    ps[:, 0:256], WF[:, i * C + f * 128: i * C + f * 128 + 128],
                    HT[:, i * 256:(i + 1) * 256],
                    start=(i == 0), stop=(i == NCH - 1)),
                    reads=["G_wf", "G_ht"], writes=[f"psb{bk}"],
                    tag=f"G_wfm{tau}_{f}_{i}")
            S.op("act", lambda e, f=f, ps=ps: e.activation(
                SV[:, f * 256:(f + 1) * 256], ps[:, 0:256], AF.Silu,
                bias=kb.vc("bf", f)),
                reads=[f"psb{bk}", f"vcol{f}"], writes=["G_sv"], tag=f"G_sl{tau}_{f}")
        if os.environ.get("MK_DBG"):
            DBROW = kb.bufs.get("DBG_ROW")
            S.op("sp", lambda e, tau=tau: e.dma_start(
                out=DBROW[tau].unsqueeze(0), in_=ROW[:]),
                reads=["G_row"], writes=["DBG_ROW"], dma=True, tag=f"G_dbrow{tau}")
            DBHT = kb.bufs.get("DBG_HT")
            S.op("pool", lambda e, tau=tau: e.dma_start(
                out=DBHT[tau], in_=HT[:]),
                reads=["G_ht"], writes=["DBG_HT"], dma=True, tag=f"G_dbht{tau}")
            DBSV = kb.bufs.get("DBG_SV")
            S.op("sp", lambda e, tau=tau: e.dma_start(
                out=DBSV[tau], in_=SV[:]),
                reads=["G_sv"], writes=["DBG_SV"], dma=True, tag=f"G_dbsv{tau}")
        ln_tail(tau, SV, "G_sv", STAU, "G_stau", "lnf_w", "lnf_b", eps_ap)
        if os.environ.get("MK_DBG"):
            DBST = kb.bufs.get("DBG_ST")
            S.op("pool", lambda e, tau=tau: e.dma_start(
                out=DBST[tau], in_=STAU[:]),
                reads=["G_stau"], writes=["DBG_ST"], dma=True, tag=f"G_dbst{tau}")
        S.op("sp", lambda e, cols=cols: e.dma_start(
            out=XT1[:].rearrange("p (i n) -> p i n", i=NCH),
            in_=X1[:, :, cols].rearrange("i p n -> p i n")),
            writes=["G_sv"], dma=True, tag=f"G_ldx1{tau}")
        S.op("sp", lambda e, cols=cols: e.dma_start(
            out=G1T[:].rearrange("p (i n) -> p i n", i=NCH),
            in_=G1[:, :, cols].rearrange("i p n -> p i n")),
            writes=["G_g1t"], dma=True, tag=f"G_ldg1{tau}")
        S.op("dve", lambda e: e.tensor_mul(OT[:], STAU[:], G1T[:]),
             reads=["G_stau", "G_g1t"], writes=["G_atl"], tag=f"G_sg{tau}")
        S.op("dve", lambda e: e.tensor_add(OT[:], OT[:], XT1[:]),
             reads=["G_atl", "G_sv"], writes=["G_atl"], tag=f"G_o{tau}")
        for half in range(2):
            for i in range(NCH):
                bk = kb.ps_bank()
                S.op("pe", lambda e, i=i, half=half, bk=bk: e.transpose(
                    kb.ps[bk][:, 0:128],
                    OT[:, i * 256 + half * 128: i * 256 + half * 128 + 128],
                    kb.identF[:]),
                    reads=["G_atl", "identF"], writes=[f"psb{bk}"],
                    tag=f"G_tp{tau}_{half}_{i}")
                S.op("dve", lambda e, i=i, bk=bk: e.tensor_copy(
                    OW[:, i * 128:(i + 1) * 128], kb.ps[bk][:, 0:128]),
                    reads=[f"psb{bk}"], writes=["G_ow"], tag=f"G_tpe{tau}_{half}_{i}")
            S.op("sp", lambda e, tau=tau, half=half: e.dma_start(
                out=OUT[tau, half * 128:half * 128 + 128, :], in_=OW[:]),
                reads=["G_ow"], writes=["OUT"], dma=True, tag=f"G_st{tau}_{half}")


def _phases():
    return [
        ("A", phaseA, [], ["XFM", "XNF", "ST0T", "ST0R"],
         ["IDENT"], dict(need_ident=True), []),
        ("B", phaseB, ["XNF"], ["QF", "KVZ"],
         ["L0_attn_W_qkv"], dict(need_ident=False), []),
        ("C", phaseC, ["QF"], ["S0"],
         ["L0_attn_W_out", "L0_attn_b_out"], dict(need_ident=False), ["KVZALL"]),
        ("D", phaseD, ["XFM", "XNF", "ST0T", "ST0R", "S0"],
         ["X1", "XN1", "ST1T", "ST1R"] +
         (["CMB0", "FFN0"] if os.environ.get("MK_DBG") else []),
         ["IDENT", "L0_ffn_Wg", "L0_ffn_bg", "L0_ffn_W1", "L0_ffn_b1",
          "L0_ffn_W2", "L0_ffn_b2"], dict(need_ident=True), []),
        ("E", phaseE, ["X1", "XN1", "ST1T", "ST1R"], ["G1"],
         ["IDENT", "L1_ffn_wrT", "L1_ffn_w1", "L1_ffn_b1", "L1_ffn_w2",
          "L1_ffn_b2"], dict(need_ident=True), []),
        ("F", phaseF, ["XN1"], ["PRr", "PRw", "PRk", "PRv", "PRa", "PRg"],
         ["VEC", "IDENT", "L1_attn_Wr", "L1_attn_Ww", "L1_attn_Wk",
          "L1_attn_Wv", "L1_attn_Wa", "L1_attn_Wg"],
         dict(need_ident=True, need_vec=True), []),
        ("G", phaseG, ["XN1", "X1", "G1", "PRr", "PRw", "PRk", "PRv", "PRa", "PRg"],
         ["OUT"] + (["DBG_Y", "DBG_YG", "DBG_AT", "DBG_ST", "DBG_HT", "DBG_SV", "DBG_ROW"]
                    if os.environ.get("MK_DBG") else []),
         ["VEC", "IDENT", "L1_attn_Wo", "L1_attn_Wf"],
         dict(need_ident=True, need_vec=True, need_b64=True), []),
    ]


STOP_AFTER = os.environ.get("MK_STOP", "")


def _run_unfused(inputs):
    per_core = host_prepare(inputs)
    store = [dict() for _ in range(NC)]
    for (name, fn, rds, wrs, wts, ckw, extra) in _phases():
        nc = bass.Bass("TRN2", target_bir_lowering=False)
        bufs = Bufs(nc, ext_in=rds, ext_out=wrs)
        kb = KB(nc, bufs, wts)
        kb.build_consts(**ckw)
        fn(kb)
        kb.S.emit()
        in_maps = []
        for c in range(NC):
            m = {k: per_core[c][k] for k in wts}
            if name == "A":
                m["x_in"] = per_core[c]["x_in"]
            for r in rds + extra:
                m[r] = store[c][r]
            in_maps.append(m)
        res = run_bass_kernel_spmd(nc, in_maps, core_ids=list(range(NC)))
        for c in range(NC):
            for wname in wrs:
                store[c][wname] = res.results[c][wname]
        if "KVZ" in wrs:
            kvall = np.stack([np.asarray(store[r]["KVZ"]) for r in range(NC)])
            for c in range(NC):
                store[c]["KVZALL"] = kvall
        if name == STOP_AFTER:
            break
    return store


def kernel(**inputs):
    store = _run_unfused(inputs)
    out = np.empty((B, NPOS, C), dtype=np.float32)
    for c in range(NC):
        out[:, c * PP:(c + 1) * PP, :] = store[c]["OUT"]
    return out
